# version 18
# baseline (speedup 1.0000x reference)
import numpy as np
import ml_dtypes
from contextlib import ExitStack

import concourse.bass as bass
import concourse.mybir as mybir
from concourse.bass_utils import run_bass_kernel_spmd
from concourse.alu_op_type import AluOpType as OP

F32 = mybir.dt.float32
BF16 = mybir.dt.bfloat16
I32 = mybir.dt.int32
AF = mybir.ActivationFunctionType
AX = mybir.AxisListType

ENGINES = ("tensor", "vector", "scalar", "gpsimd", "sync")


class Cfg:
    def __init__(self, H=16, LT=16, OIT_REAL=43, OIT_PC=6, taps=()):
        self.H = H
        self.D = 128
        self.C = 128
        self.LT = LT
        self.T = LT * 128
        self.NCH = LT * 128 // 128
        self.DM = H * 128
        self.GH = 2
        self.OIT_REAL = OIT_REAL
        self.OIT_PC = OIT_PC
        self.INTER = OIT_REAL * 128
        self.LOWR = 16
        self.taps = set(taps)
        self.LNEPS = 1e-5
        self.RMSEPS = 1e-5
        assert self.OIT_PC * 8 >= self.OIT_REAL
        assert self.H % self.GH == 0 and self.T % 512 == 0
        assert self.GH * 128 <= 512


class Tape:
    def __init__(self, nc):
        self.nc = nc
        self.ops = {e: [] for e in ENGINES}
        self.h = {}
        self.count = {}
        self.seen = {e: {} for e in ENGINES}

    def add_sem(self, name, handle):
        self.h[name] = handle
        self.count[name] = 0

    def val(self, name):
        return self.count[name]

    def emit(self, engine, fn, waits=(), incs=()):
        wl = []
        for w in waits:
            if w is None:
                continue
            (s, v) = w
            if v is None or v <= 0 or self.seen[engine].get(s, 0) >= v:
                continue
            self.seen[engine][s] = v
            wl.append((s, v))
        for (s, n) in incs:
            self.count[s] += n
        h = self.h
        wl, incs = tuple(wl), tuple(incs)

        def item(eng, fn=fn, wl=wl, incs=incs):
            for (s, v) in wl:
                eng.wait_ge(h[s], v)
            instr = fn(eng)
            for (s, n) in incs:
                assert instr is not None
                instr = instr.then_inc(h[s], n)

        self.ops[engine].append(item)
        return self.count[incs[0][0]] if incs else None

    def wait(self, engine, waits):
        self.emit(engine, lambda e_: None, waits=waits)

    def barrier(self, sems=("TE", "VE", "SE", "DS", "DA", "DG", "DS2", "DA2", "DG2")):
        snap = [(s, self.count[s]) for s in sems]
        for e in ENGINES:
            self.wait(e, snap)

    def finalize(self, block):
        for ename in ENGINES:
            def body(eng, _ops=self.ops[ename]):
                for it in _ops:
                    it(eng)
            getattr(block, ename)(body)


def build_core(cfg: Cfg):
    nc = bass.Bass(target_bir_lowering=False)
    H, D, C, LT, T, DM, GH = cfg.H, cfg.D, cfg.C, cfg.LT, cfg.T, cfg.DM, cfg.GH
    NCH, NG, LOWR = cfg.NCH, cfg.H // cfg.GH, cfg.LOWR
    OIT_PC, OIT_REAL = cfg.OIT_PC, cfg.OIT_REAL
    GHD = GH * D
    NHF = T // 4
    NSEC = 4
    WIN = 1024 if T % 1024 == 0 else 512
    WIN2 = 1024 if DM % 1024 == 0 else 512
    LNS = float(-0.5 * np.log(float(D)))

    x_in = nc.dram_tensor("x", [T, DM], F32, kind="ExternalInput").ap()
    Wq = nc.dram_tensor("Wq", [DM, DM], F32, kind="ExternalInput").ap()
    Wk = nc.dram_tensor("Wk", [DM, DM], F32, kind="ExternalInput").ap()
    Wv = nc.dram_tensor("Wv", [DM, DM], F32, kind="ExternalInput").ap()
    Wg1 = nc.dram_tensor("Wg1", [DM, LOWR], F32, kind="ExternalInput").ap()
    Wg2b = nc.dram_tensor("Wg2b", [LOWR + 1, DM], F32, kind="ExternalInput").ap()
    Wgate = nc.dram_tensor("Wgate", [DM, DM], F32, kind="ExternalInput").ap()
    norm_w = nc.dram_tensor("norm_w", [D, 1], F32, kind="ExternalInput").ap()
    Wo = nc.dram_tensor("Wo", [DM, DM], F32, kind="ExternalInput").ap()
    gidx = nc.dram_tensor("gidx", [OIT_PC * 128, DM], I32, kind="ExternalInput").ap()
    uidx = nc.dram_tensor("uidx", [OIT_PC * 128, DM], I32, kind="ExternalInput").ap()
    didxT = nc.dram_tensor("didxT", [OIT_PC * 128, DM], I32, kind="ExternalInput").ap()
    lutb = nc.dram_tensor("lutb", [128, 16], F32, kind="ExternalInput").ap()
    identc = nc.dram_tensor("identc", [128, 128], F32, kind="ExternalInput").ap()
    u64c = nc.dram_tensor("u64c", [128, C], BF16, kind="ExternalInput").ap()
    trimc = nc.dram_tensor("trimc", [128, C], BF16, kind="ExternalInput").ap()
    fmc = nc.dram_tensor("fmc", [128, 8], F32, kind="ExternalInput").ap()
    fmic = nc.dram_tensor("fmic", [128, 8], F32, kind="ExternalInput").ap()

    out = nc.dram_tensor("out", [T, DM], F32, kind="ExternalOutput").ap()

    q_sp = nc.dram_tensor("q_sp", [H, D, T], BF16, kind="Internal")
    k_sp = nc.dram_tensor("k_sp", [H, D, T], BF16, kind="Internal")
    e1_sp = nc.dram_tensor("e1_sp", [H, D, T], BF16, kind="Internal")
    kt_sp = nc.dram_tensor("kt_sp", [H, D, T], BF16, kind="Internal")
    v_sp = nc.dram_tensor("v_sp", [LT, 128, DM], BF16, kind="Internal")
    g_sp = nc.dram_tensor("g_sp", [LT, 128, DM], BF16, kind="Internal")
    tp_sp = nc.dram_tensor("tp_sp", [H, NCH, D, D], F32, kind="Internal")
    m_sp = nc.dram_tensor("m_sp", [LT, 128, DM], F32, kind="Internal")
    x2_sp = nc.dram_tensor("x2_sp", [LT, 128, DM], F32, kind="Internal")
    td_in = nc.dram_tensor("td_in", [128, H * (D + 1)], F32, kind="Internal")
    td_out = nc.dram_tensor("td_out", [8 * 128, H * (D + 1)], F32, kind="Internal")
    wg_in = nc.dram_tensor("wg_in", [OIT_PC * 128, DM], BF16, kind="Internal")
    wu_in = nc.dram_tensor("wu_in", [OIT_PC * 128, DM], BF16, kind="Internal")
    wd_in = nc.dram_tensor("wd_in", [OIT_PC * 128, DM], BF16, kind="Internal")
    wg_all = nc.dram_tensor("wg_all", [8 * OIT_PC * 128, DM], BF16, kind="Internal")
    wu_all = nc.dram_tensor("wu_all", [8 * OIT_PC * 128, DM], BF16, kind="Internal")
    wd_all = nc.dram_tensor("wd_all", [8 * OIT_PC * 128, DM], BF16, kind="Internal")

    x_t = x_in.rearrange("(lt p) d -> lt p d", p=128)
    out_t = out.rearrange("(lt p) d -> lt p d", p=128)

    ctx = ExitStack()
    e = ctx.enter_context
    t = Tape(nc)
    for s in ("TE", "VE", "SE", "GP", "DS", "DA", "DG", "DS2", "DA2", "DG2", "CC1", "CC2", "CC3", "CC4"):
        t.add_sem(s, e(nc.semaphore(s)))

    dbg = {}

    def tap(name, buf, shape, dtype, waits):
        if name not in cfg.taps:
            return
        ap = nc.dram_tensor("dbg_" + name, list(shape), dtype, kind="ExternalOutput").ap()
        t.emit("gpsimd", lambda e_: e_.dma_start(out=ap, in_=buf), waits=waits,
               incs=[("DG", 16)])
        dbg[name] = True

    def tapd(name, dram, waits):
        if name not in cfg.taps:
            return
        shape = list(dram.shape)
        ap = nc.dram_tensor("dbg_" + name, shape, dram.dtype, kind="ExternalOutput").ap()
        t.emit("gpsimd", lambda e_: e_.dma_start(out=ap, in_=dram.ap()), waits=waits,
               incs=[("DG", 16)])
        dbg[name] = True

    ident = e(nc.sbuf_tensor("ident", [128, 128], F32))
    identb = e(nc.sbuf_tensor("identb", [128, 128], BF16))
    u64 = e(nc.sbuf_tensor("u64", [128, C], BF16))
    trim = e(nc.sbuf_tensor("trim", [128, C], BF16))
    fm = e(nc.sbuf_tensor("fm", [128, 8], F32))
    fmi = e(nc.sbuf_tensor("fmi", [128, 8], F32))
    nwcol = e(nc.sbuf_tensor("nwcol", [128, 1], F32))
    lut_s = e(nc.sbuf_tensor("lut_s", [128, 16], F32))
    wg2a = e(nc.sbuf_tensor("wg2a", [32, DM], BF16))
    yTb = e(nc.sbuf_tensor("yTb", [32, T], BF16))
    stat = e(nc.sbuf_tensor("stat", [128, 16], F32))
    trash = e(nc.sbuf_tensor("trash", [128, DM], BF16))
    s_f = e(nc.sbuf_tensor("s_f", [128, D], F32))
    stmp = e(nc.sbuf_tensor("stmp", [128, D], F32))
    cst = e(nc.sbuf_tensor("cst", [128, 8], F32))

    ps = [e(nc.psum_tensor(f"ps{i}", [128, 1024], F32)) for i in range(4)]

    block = e(nc.Block())

    def dma(queue, dst, src, waits=(), ch=0, **kw):
        sem = {"sync": "DS", "scalar": "DA", "gpsimd": "DG"}[queue]
        if ch:
            sem += "2"
        w = list(waits) + [(sem, t.val(sem))]
        return t.emit(queue, lambda e_: e_.dma_start(out=dst, in_=src, **kw),
                      waits=w, incs=[(sem, 16)])

    dma("sync", ident[:], identc)
    dma("sync", u64[:], u64c)
    dma("sync", trim[:], trimc)
    dma("sync", fm[:], fmc)
    dma("sync", fmi[:], fmic)
    dma("sync", nwcol[:], norm_w)
    CONST_DS = dma("sync", lut_s[:], lutb)
    CONST_DG = dma("gpsimd", wg2a[0:LOWR + 1, :], Wg2b)
    t.emit("vector", lambda e_: e_.tensor_copy(identb[:], ident[:]),
           waits=[("DS", CONST_DS)], incs=[("VE", 1)])
    for ci, cval in ((0, 1e-5), (1, 1.0), (2, float(-0.5 * np.log(128.0))), (3, 1e-5)):
        t.emit("vector", lambda e_, ci=ci, cval=cval: e_.memset(cst[:, ci:ci + 1], cval),
               incs=[("VE", 1)])
    CST_VE = t.val("VE")

    def emit_ln(src_ap, dst_ap, width, eps):
        v1 = t.emit("vector", lambda e_: e_.tensor_reduce(
            stat[:, 0:1], src_ap, AX.X, OP.add), incs=[("VE", 1)])
        s1 = t.emit("scalar", lambda e_: e_.activation(
            trash[:, 0:width], src_ap, AF.Square, accum_out=stat[:, 1:2]),
            incs=[("SE", 1)])
        v2 = t.emit("vector", lambda e_: e_.tensor_scalar(
            stat[:, 2:3], stat[:, 0:1], 1.0 / width, None, OP.mult),
            waits=[("VE", v1), ("SE", s1)], incs=[("VE", 1)])
        v3 = t.emit("vector", lambda e_: e_.tensor_scalar(
            stat[:, 3:4], stat[:, 1:2], 1.0 / width, None, OP.mult),
            waits=[("VE", v2)], incs=[("VE", 1)])
        v4 = t.emit("vector", lambda e_: e_.tensor_tensor(
            stat[:, 4:5], stat[:, 2:3], stat[:, 2:3], OP.mult),
            waits=[("VE", v3)], incs=[("VE", 1)])
        v5 = t.emit("vector", lambda e_: e_.tensor_tensor(
            stat[:, 5:6], stat[:, 3:4], stat[:, 4:5], OP.subtract),
            waits=[("VE", v4)], incs=[("VE", 1)])
        s2 = t.emit("scalar", lambda e_: e_.activation(
            stat[:, 6:7], stat[:, 5:6], AF.Ln, bias=cst[:, 0:1]),
            waits=[("VE", v5), ("VE", CST_VE)], incs=[("SE", 1)])
        s3 = t.emit("scalar", lambda e_: e_.activation(
            stat[:, 7:8], stat[:, 6:7], AF.Exp, scale=-0.5),
            waits=[("SE", s2)], incs=[("SE", 1)])
        v6 = t.emit("vector", lambda e_: e_.tensor_tensor(
            stat[:, 8:9], stat[:, 2:3], stat[:, 7:8], OP.mult),
            waits=[("SE", s3)], incs=[("VE", 1)])
        v7 = t.emit("vector", lambda e_: e_.tensor_scalar(
            stat[:, 9:10], stat[:, 8:9], -1.0, None, OP.mult),
            waits=[("VE", v6)], incs=[("VE", 1)])
        v8 = t.emit("vector", lambda e_: e_.tensor_scalar(
            dst_ap, src_ap, stat[:, 7:8], stat[:, 9:10], OP.mult, OP.add),
            waits=[("VE", v7)], incs=[("VE", 1)])
        return v8

    tr_guard = {}
    tr_i = [0]

    def emit_tr(src_ap, dst_ap, waits, out_rows=128, extra_evac_waits=()):
        half = tr_i[0] % 2
        tr_i[0] += 1
        pslot = ps[3][0:out_rows, half * 512: half * 512 + 128]
        w = list(waits)
        g = tr_guard.get(half)
        if g:
            w.append(g)
        t.emit("tensor", lambda e_: e_.transpose(pslot, src_ap, ident[:]),
               waits=w, incs=[("TE", 1)])
        tev = t.val("TE")
        ev = t.emit("scalar", lambda e_: e_.copy(dst_ap, pslot),
                    waits=[("TE", tev)] + list(extra_evac_waits), incs=[("SE", 1)])
        tr_guard[half] = ("SE", ev)
        return ev

    with ExitStack() as phAttn:
        eAt = phAttn.enter_context
        dec = eAt(nc.sbuf_tensor("dec", [128, H * NCH], F32))
        sin_f = eAt(nc.sbuf_tensor("sin_f", [128, H * D], F32))
        mat_i = eAt(nc.sbuf_tensor("mat_i", [128, DM], BF16))
        mat_a = eAt(nc.sbuf_tensor("mat_a", [128, DM], BF16))
        mat_k = eAt(nc.sbuf_tensor("mat_k", [128, DM], BF16))
        mat_s = eAt(nc.sbuf_tensor("mat_s", [128, DM], BF16))
        mtmp = eAt(nc.sbuf_tensor("mtmp", [128, 128], F32))
        mprev = {"store": 0, "chain": 0, "tr_se": 0}
        MQ = [(gidx, wg_in, True, ot) for ot in range(OIT_PC)] + \
             [(uidx, wu_in, True, ot) for ot in range(OIT_PC)] + \
             [(didxT, wd_in, False, ot) for ot in range(OIT_PC)]

        GATHER_OF = [(wg_in, wg_all, "CC2"), (wu_in, wu_all, "CC3"),
                     (wd_in, wd_all, "CC4")]

        def _m_generator():
            for idx_ap, in_dram, transpose, ot in MQ:
                ld = dma("gpsimd", mat_i[:], idx_ap[ot * 128:(ot + 1) * 128, :],
                         waits=[("VE", mprev["chain"])], ch=ot % 2)
                sem_ld = "DG2" if ot % 2 else "DG"
                va = t.emit("vector", lambda e_: e_.tensor_scalar(
                    mat_a[:], mat_i[:], 0.0, lut_s[:, 0:1], OP.is_equal, OP.mult),
                    waits=[(sem_ld, ld), ("DS", CONST_DS), ("DS", mprev["store"]),
                           ("DA", mprev["store"]), ("SE", mprev["tr_se"])],
                    incs=[("VE", 1)])
                yield
                for jj in range(1, 16):
                    vk = t.emit("vector", lambda e_, jj=jj: e_.tensor_scalar(
                        mat_k[:], mat_i[:], float(jj), lut_s[:, jj:jj + 1],
                        OP.is_equal, OP.mult),
                        waits=[("VE", va)], incs=[("VE", 1)])
                    va = t.emit("vector", lambda e_: e_.tensor_tensor(
                        mat_a[:], mat_a[:], mat_k[:], OP.add),
                        waits=[("VE", vk)], incs=[("VE", 1)])
                    yield
                mprev["chain"] = va
                if not transpose:
                    mprev["store"] = dma("scalar", in_dram.ap()[ot * 128:(ot + 1) * 128, :],
                                         mat_a[:], waits=[("VE", va)])
                    yield
                else:
                    for kt in range(H):
                        vv = t.emit("vector", lambda e_, kt=kt: e_.tensor_copy(
                            mtmp[:], mat_a[:, kt * 128:(kt + 1) * 128]),
                            waits=[("VE", va), ("TE", t.val("TE"))],
                            incs=[("VE", 1)])
                        emit_tr(mtmp[:], mat_s[:, kt * 128:(kt + 1) * 128],
                                [("VE", vv)], extra_evac_waits=[("DA", mprev["store"]),
                                                                ("DS", mprev["store"])])
                        if kt % 4 == 3:
                            yield
                    mprev["tr_se"] = t.val("SE")
                    mprev["chain"] = t.val("VE")
                    mprev["store"] = dma("scalar", in_dram.ap()[ot * 128:(ot + 1) * 128, :],
                                         mat_s[:], waits=[("SE", mprev["tr_se"])])
                    yield
                if ot == OIT_PC - 1:
                    for (g_in, g_all, g_sem) in GATHER_OF:
                        if g_in is in_dram:
                            t.emit("gpsimd", lambda e_, g_in=g_in, g_all=g_all:
                                   e_.collective_compute(
                                       "AllGather", OP.bypass,
                                       replica_groups=[list(range(8))],
                                       ins=[g_in.ap()], outs=[g_all.ap()]),
                                   waits=[("DA", mprev["store"]),
                                          ("DS", mprev["store"])],
                                   incs=[(g_sem, 1)])
                    yield
            while True:
                yield

        _MG = _m_generator()

        def emit_mtile(n=3):
            for _ in range(n):
                next(_MG)

        ln1T_scope = ExitStack()
        ln1T = ln1T_scope.enter_context(nc.sbuf_tensor("ln1T", [128, H * T], BF16))

        with ExitStack() as ph0:
            e0 = ph0.enter_context
            xslab = e0(nc.sbuf_tensor("xslab", [128, 2 * DM], F32))
            ln1s = e0(nc.sbuf_tensor("ln1s", [128, 2 * DM], F32))
            wslab = e0(nc.sbuf_tensor("wslab", [128, H * WIN2], BF16))
            wg1s = e0(nc.sbuf_tensor("wg1s", [128, H * LOWR], BF16))
            yT = e0(nc.sbuf_tensor("yT", [32, T], F32))
            evb = e0(nc.sbuf_tensor("evb", [128, 2 * DM], BF16))

            apply_v = {}
            tr_after = {}
            for lt in range(LT):
                par = lt % 2
                xb = xslab[:, par * DM:(par + 1) * DM]
                w = [("DS", CONST_DS)]
                if lt >= 2:
                    w.append(("VE", apply_v[lt - 2]))
                xl = dma("sync", xb, x_t[lt], waits=w)
                t.wait("vector", [("DS", xl)])
                t.wait("scalar", [("DS", xl)])
                if lt >= 2:
                    t.wait("vector", [("TE", tr_after[lt - 2])])
                lv = emit_ln(xb, ln1s[:, par * DM:(par + 1) * DM], DM, cfg.LNEPS)
                apply_v[lt] = lv
                for kt in range(H):
                    emit_tr(ln1s[:, par * DM + kt * 128: par * DM + (kt + 1) * 128],
                            ln1T[:, kt * T + lt * 128: kt * T + (lt + 1) * 128],
                            [("VE", lv)])
                tr_after[lt] = t.val("TE")
            LN1T_SE = t.val("SE")
            tap("ln1T", ln1T[:], (128, H * T), BF16, [("SE", LN1T_SE)])

            dma("gpsimd", wg1s[:].rearrange("p (kt r) -> p kt r", kt=H),
                Wg1.rearrange("(kt p) r -> p kt r", p=128))
            G_LD = t.val("DG")
            yg = {}
            for nq in range(T // 512):
                half = nq % 2
                w0 = [("DG", G_LD), ("SE", LN1T_SE)]
                if nq >= 2:
                    w0.append(yg[nq - 2])
                for kt in range(H):
                    t.emit("tensor", lambda e_, kt=kt, nq=nq, half=half: e_.matmul(
                        ps[0][0:LOWR, half * 512:half * 512 + 512],
                        wg1s[:, kt * LOWR:(kt + 1) * LOWR],
                        ln1T[:, kt * T + nq * 512: kt * T + (nq + 1) * 512],
                        start=(kt == 0), stop=(kt == H - 1)),
                        waits=w0 if kt == 0 else [],
                        incs=[("TE", 1)] if kt == H - 1 else [])
                tev = t.val("TE")
                ev = t.emit("scalar", lambda e_, nq=nq, half=half: e_.copy(
                    yT[0:LOWR, nq * 512:(nq + 1) * 512],
                    ps[0][0:LOWR, half * 512:half * 512 + 512]),
                    waits=[("TE", tev)], incs=[("SE", 1)])
                yg[nq] = ("SE", ev)
            vms = t.emit("vector", lambda e_: e_.memset(yTb[0:32, :], 1.0),
                         incs=[("VE", 1)])
            YT_VE = t.emit("vector", lambda e_: e_.tensor_copy(
                yTb[0:LOWR, :], yT[0:LOWR, :]),
                waits=[("SE", t.val("SE")), ("VE", vms)], incs=[("VE", 1)])
            tap("yT", yT[0:LOWR, :], (LOWR, T), F32, [("SE", t.val("SE"))])

            def proj_tm(W_ap, spill):
                NW = DM // WIN2
                NHN = WIN2 // 512
                ev_guard = {0: None, 1: None}
                sp_guard = {0: None, 1: None}
                entry = [("SE", t.val("SE")), ("VE", t.val("VE")), ("DS", t.val("DS"))]
                for win in range(NW):
                    wl = dma("gpsimd", wslab[:, 0:H * WIN2].rearrange("p (kt c) -> p kt c", kt=H),
                             W_ap.rearrange("(kt p) c -> p kt c", p=128)[:, :, win * WIN2:(win + 1) * WIN2],
                             waits=[("TE", t.val("TE"))])
                    for lt in range(LT):
                        pb = lt % 2
                        w0 = [("DG", wl), ("SE", LN1T_SE)] + entry
                        if ev_guard[pb]:
                            w0.append(ev_guard[pb])
                        for kt in range(H):
                            for hn in range(NHN):
                                t.emit("tensor", lambda e_, kt=kt, lt=lt, hn=hn, pb=pb: e_.matmul(
                                    ps[pb][:, hn * 512:(hn + 1) * 512],
                                    ln1T[:, kt * T + lt * 128: kt * T + (lt + 1) * 128],
                                    wslab[:, kt * WIN2 + hn * 512: kt * WIN2 + (hn + 1) * 512],
                                    start=(kt == 0), stop=(kt == H - 1)),
                                    waits=w0 if (kt == 0 and hn == 0) else [],
                                    incs=[("TE", 1)] if (kt == H - 1 and hn == NHN - 1) else [])
                        tev = t.val("TE")
                        evw = [("TE", tev)] + entry
                        if sp_guard[pb]:
                            evw.append(sp_guard[pb])
                        evv = t.emit("vector", lambda e_, lt=lt, win=win, pb=pb: e_.tensor_copy(
                            evb[:, pb * DM + win * WIN2: pb * DM + (win + 1) * WIN2],
                            ps[pb][:, 0:WIN2]),
                            waits=evw, incs=[("VE", 1)])
                        ev_guard[pb] = ("VE", evv)
                        sd = dma("sync",
                                 spill.ap()[lt][:, win * WIN2:(win + 1) * WIN2],
                                 evb[:, pb * DM + win * WIN2: pb * DM + (win + 1) * WIN2],
                                 waits=[("VE", evv)])
                        sp_guard[pb] = ("DS", sd)
                return t.val("DS")

            proj_tm(Wv, v_sp)
            proj_tm(Wgate, g_sp)
            t.barrier()
            tapd("v_sp", v_sp, [("DS", t.val("DS"))])
            tapd("g_sp", g_sp, [("DS", t.val("DS"))])

        def load_chunk_major(dst, spill, g0, waits):
            last = None
            src3 = spill.ap().rearrange("lt p c -> p lt c")
            for hlf in range(2):
                last = dma("sync",
                           dst[:].rearrange("p (j c) -> p j c", j=NCH)
                           [:, hlf * (NCH // 2):(hlf + 1) * (NCH // 2), :],
                           src3[:, hlf * (NCH // 2):(hlf + 1) * (NCH // 2),
                                g0 * GHD:(g0 + 1) * GHD],
                           waits=waits)
            return last

        with ExitStack() as phA0:
            eA0 = phA0.enter_context
            wfull = eA0(nc.sbuf_tensor("wfull", [128, H * DM], BF16))
            pstg = eA0(nc.sbuf_tensor("pstg", [128, 2 * WIN], BF16))
            NHN0 = WIN // 512
            A0_entry = [("SE", t.val("SE")), ("VE", t.val("VE"))]
            pj_guard = {0: None, 1: None}
            st_guard = {0: None, 1: None}
            for (Wap, spill) in ((Wq, q_sp), (Wk, k_sp)):
                wprev_te = t.val("TE")
                wl = 0
                for kb in range(4):
                    wl = dma("gpsimd",
                             wfull[:, kb * (H // 4) * DM:(kb + 1) * (H // 4) * DM]
                             .rearrange("p (kt c) -> p kt c", kt=H // 4),
                             Wap.rearrange("(kt p) c -> p kt c", p=128)
                             [:, kb * (H // 4):(kb + 1) * (H // 4), :],
                             waits=[("TE", wprev_te)], ch=kb % 2)
                wl = t.val("DG")
                wl2 = t.val("DG2")
                for mt in range(H):
                    for win in range(T // WIN):
                        pb = (mt * (T // WIN) + win) % 2
                        w0 = [("DG", wl), ("DG2", wl2), ("SE", LN1T_SE)] + A0_entry
                        if pj_guard[pb]:
                            w0.append(pj_guard[pb])
                        for kt in range(H):
                            for hn in range(NHN0):
                                t.emit("tensor", lambda e_, kt=kt, mt=mt, win=win, hn=hn, pb=pb: e_.matmul(
                                    ps[pb][:, hn * 512:(hn + 1) * 512],
                                    wfull[:, kt * DM + mt * 128: kt * DM + (mt + 1) * 128],
                                    ln1T[:, kt * T + win * WIN + hn * 512:
                                         kt * T + win * WIN + (hn + 1) * 512],
                                    start=(kt == 0), stop=(kt == H - 1)),
                                    waits=w0 if (kt == 0 and hn == 0) else [],
                                    incs=[("TE", 1)] if (kt == H - 1 and hn == NHN0 - 1) else [])
                        tev = t.val("TE")
                        evw = [("TE", tev)]
                        if st_guard[pb]:
                            evw.append(st_guard[pb])
                        evv = t.emit("vector", lambda e_, pb=pb: e_.tensor_copy(
                            pstg[:, pb * WIN:(pb + 1) * WIN], ps[pb][:, 0:WIN]),
                            waits=evw, incs=[("VE", 1)])
                        pj_guard[pb] = ("VE", evv)
                        sd = dma("sync", spill.ap()[mt][:, win * WIN:(win + 1) * WIN],
                                 pstg[:, pb * WIN:(pb + 1) * WIN],
                                 waits=[("VE", evv)], ch=pb)
                        st_guard[pb] = ("DS2" if pb else "DS", sd)
                        emit_mtile(12)
            emit_mtile(500)
            t.barrier()

        with ExitStack() as phA:
            eA = phA.enter_context
            tdst = eA(nc.sbuf_tensor("tdst", [128, H * (D + 1)], F32))
            hk = eA(nc.sbuf_tensor("hk", [128, T], BF16))
            Gg = eA(nc.sbuf_tensor("Gg", [128, GH * T], F32))
            sps = eA(nc.sbuf_tensor("sps", [128, LT * GH * 128], BF16))
            spf = eA(nc.sbuf_tensor("spf", [128, 2 * GH * 128], F32))
            e1b = eA(nc.sbuf_tensor("e1b", [128, GH * T], BF16))
            e6b = eA(nc.sbuf_tensor("e6b", [128, GH * T], BF16))
            ktb = eA(nc.sbuf_tensor("ktb", [128, T], BF16))
            ktm = eA(nc.sbuf_tensor("ktm", [128, NCH * D], BF16))
            ktm2 = eA(nc.sbuf_tensor("ktm2", [128, NCH * D], BF16))
            ktb2 = eA(nc.sbuf_tensor("ktb2", [128, T], F32))
            ktf = eA(nc.sbuf_tensor("ktf", [128, T], F32))
            glcA = eA(nc.sbuf_tensor("glcA", [128, GH * 32], F32))
            glcB = eA(nc.sbuf_tensor("glcB", [128, GH * 32], F32))
            sdecS = eA(nc.sbuf_tensor("sdecS", [128, GH * 32], F32))
            vgc = eA(nc.sbuf_tensor("vgc", [128, NCH * GHD], BF16))
            tps = eA(nc.sbuf_tensor("tps", [128, 8 * D], F32))

            prev_sp = {"e1": 0, "kt": 0, "q": 0, "tp": 0, "vf": 0}
            ps1_guard = {}
            prev_h4_te = 0
            g_end_te = 0
            g_end_ve = 0
            g_end_se = 0
            for g0 in range(NG):
                heads = [g0 * GH + i for i in range(GH)]
                VL = load_chunk_major(vgc, v_sp, g0, [("TE", g_end_te)])

                zg = {}
                zev = 0
                W = GH * 128
                ZPB = max(1, 512 // W)
                for lt in range(LT):
                    bank = (lt // ZPB) % 2
                    w0 = [("VE", YT_VE), ("DG", CONST_DG), ("TE", g_end_te),
                          ("SE", g_end_se), ("VE", g_end_ve)]
                    g = zg.get(bank)
                    if lt % ZPB == 0 and g:
                        w0.append(g)
                    t.emit("tensor", lambda e_, lt=lt, bank=bank, g0=g0: e_.matmul(
                        ps[2][:, bank * 512 + (lt % ZPB) * W:
                              bank * 512 + (lt % ZPB) * W + W],
                        yTb[0:LOWR + 1, lt * 128:(lt + 1) * 128],
                        wg2a[0:LOWR + 1, g0 * GH * 128:(g0 + 1) * GH * 128],
                        start=True, stop=True),
                        waits=w0 if lt % ZPB == 0 else [],
                        incs=[("TE", 1)] if (lt % ZPB == ZPB - 1 or lt == LT - 1) else [])
                    if lt % ZPB == ZPB - 1 or lt == LT - 1:
                        tev = t.val("TE")
                        lt0 = (lt // ZPB) * ZPB
                        nw = (lt - lt0 + 1) * W
                        zev = t.emit("vector", lambda e_, lt0=lt0, bank=bank, nw=nw: e_.tensor_copy(
                            Gg[:, lt0 * W:lt0 * W + nw],
                            ps[2][:, bank * 512: bank * 512 + nw]),
                            waits=[("TE", tev), ("VE", g_end_ve)], incs=[("VE", 1)])
                        zg[bank] = ("VE", zev)
                ZW = LT * GH * 128
                sa = t.emit("scalar", lambda e_: e_.activation(
                    Gg[:, 0:ZW], Gg[:, 0:ZW], AF.Sigmoid),
                    waits=[("VE", zev), ("SE", g_end_se)], incs=[("SE", 1)])
                t.emit("scalar", lambda e_: e_.activation(
                    sps[:, 0:ZW], Gg[:, 0:ZW], AF.Ln),
                    waits=[("SE", sa)], incs=[("SE", 1)])
                SP_SE = t.val("SE")

                gslot = {}
                GQ = min(4, NCH)
                for qi in range((GH * NCH) // GQ):
                    bank = qi % 2
                    w0 = [("SE", SP_SE), ("DS", CONST_DS), ("VE", g_end_ve)]
                    g = gslot.get(bank)
                    if g:
                        w0.append(g)
                    for jj in range(GQ):
                        idx = qi * GQ + jj
                        i, j = idx // NCH, idx % NCH
                        t.emit("tensor", lambda e_, i=i, j=j, bank=bank, jj=jj: e_.matmul(
                            ps[0][0:128, bank * 512 + jj * C: bank * 512 + jj * C + C],
                            sps[:, j * GH * 128 + i * 128:j * GH * 128 + (i + 1) * 128],
                            u64[:, :],
                            start=True, stop=True),
                            waits=w0 if jj == 0 else [],
                            incs=[("TE", 1)] if jj == GQ - 1 else [])
                    tev = t.val("TE")
                    i0, j0 = (qi * GQ) // NCH, (qi * GQ) % NCH
                    ev = t.emit("vector", lambda e_, i0=i0, j0=j0, bank=bank: e_.tensor_copy(
                        Gg[:, i0 * T + j0 * C: i0 * T + (j0 + GQ) * C],
                        ps[0][0:128, bank * 512: bank * 512 + GQ * C]),
                        waits=[("TE", tev)], incs=[("VE", 1)])
                    gslot[bank] = ("VE", ev)
                G_SE = t.val("SE")
                G_VE = t.val("VE")

                sdec = t.emit("scalar", lambda e_, h0=heads[0], h1=heads[-1] + 1: e_.activation(
                    dec[:, h0 * NCH:h1 * NCH],
                    Gg[:, C - 1: GH * T: C][:, 0:GH * NCH], AF.Exp),
                    waits=[("VE", G_VE)], incs=[("SE", 1)])
                GW = GH * 32
                m1 = t.emit("vector", lambda e_: e_.memset(glcA[:], 0.0),
                            waits=[("SE", g_end_se), ("VE", g_end_ve)], incs=[("VE", 1)])
                m2 = t.emit("vector", lambda e_: e_.memset(glcB[:], 0.0),
                            waits=[("SE", g_end_se)], incs=[("VE", 1)])
                for i in range(GH):
                    t.emit("vector", lambda e_, i=i: e_.tensor_copy(
                        glcA[:, i * 32:i * 32 + NCH],
                        Gg[:, i * T + C - 1: i * T + T: C][:, 0:NCH]),
                        waits=[("VE", G_VE), ("VE", m1)], incs=[("VE", 1)])
                cur, oth = glcA, glcB
                for sstep in (1, 2, 4, 8):
                    t.emit("vector", lambda e_, cur=cur, oth=oth, sstep=sstep: e_.tensor_tensor(
                        oth[:, 0:GW - sstep], cur[:, 0:GW - sstep],
                        cur[:, sstep:GW], OP.add),
                        waits=[("VE", t.val("VE"))], incs=[("VE", 1)])
                    cur, oth = oth, cur
                SC_VE = t.val("VE")
                sdecS_se = t.emit("scalar", lambda e_, cur=cur: e_.activation(
                    sdecS[:], cur[:], AF.Exp),
                    waits=[("VE", SC_VE)], incs=[("SE", 1)])

                se1 = t.emit("scalar", lambda e_: e_.activation(
                    e1b[:, 0:GH * T], Gg[:, 0:GH * T], AF.Exp, bias=cst[:, 2:3]),
                    waits=[("VE", G_VE), ("DS", prev_sp["e1"]), ("VE", CST_VE)],
                    incs=[("SE", 1)])
                for i, h in enumerate(heads):
                    prev_sp["e1"] = dma("sync", e1_sp.ap()[h],
                                        e1b[:, i * T:(i + 1) * T],
                                        waits=[("SE", se1)])
                se6 = t.emit("scalar", lambda e_: e_.activation(
                    e6b[:, 0:GH * T], Gg[:, 0:GH * T], AF.Exp, scale=-1.0),
                    waits=[("SE", se1)], incs=[("SE", 1)])

                for i, h in enumerate(heads):
                    t.emit("vector", lambda e_, i=i, h=h: e_.tensor_copy(
                        tdst[:, h * (D + 1) + D: h * (D + 1) + D + 1],
                        sdecS[:, i * 32:i * 32 + 1]),
                        waits=[("SE", sdecS_se)], incs=[("VE", 1)])

                    kl = dma("sync", hk[:], k_sp.ap()[h],
                             waits=[("VE", prev_sp["vf"])], ch=1)
                    vf = t.emit("vector", lambda e_, i=i: e_.tensor_tensor(
                        ktf[:], hk[:], e6b[:, i * T:(i + 1) * T], OP.mult),
                        waits=[("DS2", kl), ("SE", se6), ("TE", g_end_te)],
                        incs=[("VE", 1)])
                    prev_sp["vf"] = vf
                    vb2 = t.emit("vector", lambda e_: e_.tensor_copy(ktb[:], ktf[:]),
                                 waits=[("VE", vf), ("DS", prev_sp["kt"])],
                                 incs=[("VE", 1)])
                    prev_sp["kt"] = dma("sync", kt_sp.ap()[h], ktb[:],
                                        waits=[("VE", vb2)])

                    for j in range(NCH):
                        t.emit("vector", lambda e_, j=j, i=i: e_.tensor_scalar(
                            ktb2[:, j * C:(j + 1) * C], ktf[:, j * C:(j + 1) * C],
                            sdecS[:, i * 32 + j: i * 32 + j + 1], None, OP.mult),
                            waits=[("VE", vf), ("SE", sdecS_se),
                                   ("TE", prev_h4_te)] if j == 0 else [],
                            incs=[("VE", 1)] if j == NCH - 1 else [])
                    KS_VE = t.val("VE")
                    NQ = (NCH + 3) // 4
                    tr_g = {}
                    for q in range(NQ):
                        bank = q % 2
                        nq = min(4, NCH - q * 4)
                        trw = [("VE", vf), ("VE", G_VE)]
                        g = tr_g.get(bank)
                        if g:
                            trw.append(g)
                        for jj in range(nq):
                            j = q * 4 + jj
                            t.emit("tensor", lambda e_, j=j, bank=bank, jj=jj: e_.transpose(
                                ps[3][:, bank * 512 + jj * 128: bank * 512 + jj * 128 + 128],
                                ktf[:, j * C:(j + 1) * C], ident[:]),
                                waits=trw if jj == 0 else [],
                                incs=[("TE", 1)] if jj == nq - 1 else [])
                        tev = t.val("TE")
                        ev = t.emit("scalar", lambda e_, q=q, bank=bank, nq=nq: e_.copy(
                            ktm[:, q * 4 * D: q * 4 * D + nq * D],
                            ps[3][:, bank * 512: bank * 512 + nq * 128]),
                            waits=[("TE", tev), ("TE", prev_h4_te)], incs=[("SE", 1)])
                        tr_g[bank] = ("SE", ev)
                    TR_SE = t.val("SE")
                    trp_ev = {}
                    trp_g = {}
                    for q in range(NQ):
                        bank = q % 2
                        nq = min(4, NCH - q * 4)
                        trw = [("VE", KS_VE), ("VE", G_VE)]
                        g = trp_g.get(bank)
                        if g:
                            trw.append(g)
                        for jj in range(nq):
                            j = q * 4 + jj
                            t.emit("tensor", lambda e_, j=j, bank=bank, jj=jj: e_.transpose(
                                ps[0][:, bank * 512 + jj * 128: bank * 512 + jj * 128 + 128],
                                ktb2[:, j * C:(j + 1) * C], ident[:]),
                                waits=trw if jj == 0 else [],
                                incs=[("TE", 1)] if jj == nq - 1 else [])
                        tev = t.val("TE")
                        ev = t.emit("vector", lambda e_, q=q, bank=bank, nq=nq: e_.tensor_copy(
                            ktm2[:, q * 4 * D: q * 4 * D + nq * D],
                            ps[0][:, bank * 512: bank * 512 + nq * 128]),
                            waits=[("TE", tev), ("TE", prev_h4_te)], incs=[("VE", 1)])
                        trp_ev[q] = ("VE", t.val("VE"))
                        trp_g[bank] = ("VE", t.val("VE"))
                    FSL = [(2, 0), (2, 1)]
                    tp_guard = {}
                    for q in range(NQ):
                        pi, hf = FSL[q % 2]
                        nq = min(4, NCH - q * 4)
                        w0 = [("SE", TR_SE), ("DS", VL)]
                        g = tp_guard.get(q % 2)
                        if g:
                            w0.append(g)
                        for jj in range(nq):
                            j = q * 4 + jj
                            t.emit("tensor", lambda e_, j=j, pi=pi, hf=hf, jj=jj, i=i: e_.matmul(
                                ps[pi][0:D, hf * 512 + jj * D: hf * 512 + (jj + 1) * D],
                                ktm[:, j * D:(j + 1) * D],
                                vgc[:, j * GHD + i * D: j * GHD + (i + 1) * D],
                                start=True, stop=True),
                                waits=w0 if jj == 0 else [],
                                incs=[("TE", 1)] if jj == nq - 1 else [])
                        tev = t.val("TE")
                        sq = q % 2
                        ev2 = t.emit("scalar", lambda e_, sq=sq, pi=pi, hf=hf, nq=nq: e_.copy(
                            tps[:, sq * 512: sq * 512 + nq * D],
                            ps[pi][0:D, hf * 512: hf * 512 + nq * D]),
                            waits=[("TE", tev), ("DS", prev_sp["tp"])], incs=[("SE", 1)])
                        tp_guard[q % 2] = ("SE", ev2)
                        if q % 2 == 1 or q == NQ - 1:
                            j = q * 4 + nq - 1
                            j0 = (j // 8) * 8
                            nj = j - j0 + 1
                            prev_sp["tp"] = dma(
                                "sync",
                                tp_sp.ap()[h][j0:j + 1].rearrange("j a b -> a j b"),
                                tps[:, 0:nj * D].rearrange("a (j b) -> a j b", j=nj),
                                waits=[("SE", t.val("SE"))])
                    abank = i % 2
                    for j in range(NCH):
                        w0 = []
                        if j % 4 == 0:
                            w0 = [trp_ev[j // 4]]
                        if j == 0:
                            w0 = w0 + [("DS", VL)]
                            g = ps1_guard.get(abank)
                            if g:
                                w0.append(g)
                        t.emit("tensor", lambda e_, j=j, abank=abank, i=i: e_.matmul(
                            ps[1][0:D, abank * 512: abank * 512 + D],
                            ktm2[:, j * D:(j + 1) * D],
                            vgc[:, j * GHD + i * D: j * GHD + (i + 1) * D],
                            start=(j == 0), stop=(j == NCH - 1)),
                            waits=w0,
                            incs=[("TE", 1)] if j == NCH - 1 else [])
                    acc_te = t.val("TE")
                    sfv = t.emit("vector", lambda e_, h=h, abank=abank: e_.tensor_copy(
                        tdst[:, h * (D + 1): h * (D + 1) + D],
                        ps[1][0:D, abank * 512: abank * 512 + D]),
                        waits=[("TE", acc_te)], incs=[("VE", 1)])
                    ps1_guard[abank] = ("VE", sfv)
                    prev_h4_te = t.val("TE")
                g_end_te = t.val("TE")
                g_end_ve = t.val("VE")
                g_end_se = t.val("SE")
            TD_VE, TD_SE = t.val("VE"), t.val("SE")
            tap("tdst", tdst[:], (128, H * (D + 1)), F32,
                [("VE", TD_VE), ("SE", TD_SE)])
            tdw = dma("sync", td_in.ap(), tdst[:], waits=[("VE", TD_VE), ("SE", TD_SE)])
            t.emit("gpsimd", lambda e_: e_.collective_compute(
                "AllGather", OP.bypass, replica_groups=[list(range(8))],
                ins=[td_in.ap()], outs=[td_out.ap()]),
                waits=[("DS", tdw)], incs=[("CC1", 1)])
            t.barrier()
            tapd("q_sp", q_sp, [("DS", t.val("DS"))])
            tapd("kt_sp", kt_sp, [("DS", t.val("DS"))])
            tapd("e1_sp", e1_sp, [("DS", t.val("DS"))])
            tapd("tp_sp", tp_sp, [("DS", t.val("DS"))])
        ln1T_scope.close()

        with ExitStack() as phB:
            eB = phB.enter_context
            tdr = eB(nc.sbuf_tensor("tdr", [128, 8 * H * (D + 1)], F32))
            dmc = eB(nc.sbuf_tensor("dmc", [128, 8 * H], F32))
            tmsk = eB(nc.sbuf_tensor("tmsk", [128, D], F32))

            RW = H * (D + 1)
            tl = dma("sync", tdr[:].rearrange("p (r c) -> p r c", r=8),
                     td_out.ap().rearrange("(r p) c -> p r c", p=128),
                     waits=[("CC1", 1)])
            t.wait("vector", [("DS", tl)])
            for r in range(8):
                for h in range(H):
                    v1 = t.emit("vector", lambda e_, r=r, h=h: e_.tensor_tensor(
                        dmc[:, r * H + h: r * H + h + 1],
                        tdr[:, r * RW + h * (D + 1) + D: r * RW + h * (D + 1) + D + 1],
                        fm[:, r:r + 1], OP.mult), incs=[("VE", 1)])
                    t.emit("vector", lambda e_, r=r, h=h: e_.tensor_tensor(
                        dmc[:, r * H + h: r * H + h + 1],
                        dmc[:, r * H + h: r * H + h + 1],
                        fmi[:, r:r + 1], OP.add),
                        waits=[("VE", v1)], incs=[("VE", 1)])
            DMC_VE = t.val("VE")
            t.wait("vector", [("VE", DMC_VE)])
            for h in range(H):
                cv = t.emit("vector", lambda e_, h=h: e_.memset(
                    sin_f[:, h * D:(h + 1) * D], 0.0), incs=[("VE", 1)])
                for r in range(8):
                    c1 = t.emit("vector", lambda e_, r=r, h=h: e_.tensor_scalar(
                        sin_f[:, h * D:(h + 1) * D], sin_f[:, h * D:(h + 1) * D],
                        dmc[:, r * H + h: r * H + h + 1], None, OP.mult),
                        waits=[("VE", cv)], incs=[("VE", 1)])
                    c2 = t.emit("vector", lambda e_, r=r, h=h: e_.tensor_scalar(
                        tmsk[:], tdr[:, r * RW + h * (D + 1): r * RW + h * (D + 1) + D],
                        fm[:, r:r + 1], None, OP.mult),
                        waits=[("VE", c1)], incs=[("VE", 1)])
                    cv = t.emit("vector", lambda e_, h=h: e_.tensor_tensor(
                        sin_f[:, h * D:(h + 1) * D], sin_f[:, h * D:(h + 1) * D],
                        tmsk[:], OP.add),
                        waits=[("VE", c2)], incs=[("VE", 1)])
            SIN_VE = t.val("VE")
            tap("sin", sin_f[:], (128, H * D), F32, [("VE", SIN_VE)])
            t.barrier()

        with ExitStack() as phC:
            eC = phC.enter_context
            hq2 = eC(nc.sbuf_tensor("hq2", [128, 2 * T], BF16))
            he1 = eC(nc.sbuf_tensor("he1", [128, 2 * T], BF16))
            hkt = eC(nc.sbuf_tensor("hkt", [128, 2 * T], BF16))
            hqg = eC(nc.sbuf_tensor("hqg", [128, 2 * T], BF16))
            vgc2 = eC(nc.sbuf_tensor("vgc2", [128, NCH * GHD], BF16))
            ggc = eC(nc.sbuf_tensor("ggc", [128, NCH * GHD], BF16))
            tpr = eC(nc.sbuf_tensor("tpr", [128, 2 * 8 * D], F32))
            sbj = eC(nc.sbuf_tensor("sbj", [128, NCH * D], BF16))
            amb = eC(nc.sbuf_tensor("amb", [128, NCH * C], BF16))
            ost = eC(nc.sbuf_tensor("ost", [128, NCH * GHD], F32))
            rst = eC(nc.sbuf_tensor("rst", [128, 2 * NCH * GH], F32))
            sgb = eC(nc.sbuf_tensor("sgb", [128, NCH * GHD], BF16))
            mst = eC(nc.sbuf_tensor("mst", [128, NCH * GHD], F32))

            prev_norm = ("VE", 0)
            prev_norm_se = ("SE", 0)
            prev_mspill = 0
            g_end_te2 = 0
            for g0 in range(NG):
                heads = [g0 * GH + i for i in range(GH)]
                VL = load_chunk_major(vgc2, v_sp, g0, [("TE", g_end_te2)])
                GL = load_chunk_major(ggc, g_sp, g0, [("SE", prev_norm_se[1]),
                                                      ("VE", prev_norm[1])])
                o_guard = {}
                a_guard = {}
                amb_guard = {}
                h_end_te = g_end_te2
                hprev = {0: g_end_te2, 1: g_end_te2}
                prev_ch_ve = t.val("VE")
                for i, h in enumerate(heads):
                    hp = i % 2
                    off = hp * T
                    ql = dma("scalar", hq2[:, off:off + T], q_sp.ap()[h],
                             waits=[("TE", hprev[hp])], ch=0)
                    ql2 = dma("scalar", he1[:, off:off + T], e1_sp.ap()[h],
                              waits=[("TE", hprev[hp])], ch=1)
                    ql3 = dma("sync", hkt[:, off:off + T], kt_sp.ap()[h],
                              waits=[("TE", hprev[hp])], ch=1)
                    vqg = t.emit("vector", lambda e_, off=off: e_.tensor_tensor(
                        hqg[:, off:off + T], hq2[:, off:off + T], he1[:, off:off + T],
                        OP.mult),
                        waits=[("DA", ql), ("DA2", ql2), ("DS2", ql3),
                               ("VE", SIN_VE), ("TE", hprev[hp])],
                        incs=[("VE", 1)])
                    vcp = t.emit("vector", lambda e_, h=h: e_.tensor_copy(
                        s_f[:], sin_f[:, h * D:(h + 1) * D]),
                        waits=[("VE", SIN_VE)], incs=[("VE", 1)])
                    ch_v = t.emit("vector", lambda e_: e_.tensor_copy(
                        sbj[:, 0:D], s_f[:]),
                        waits=[("VE", vcp), ("TE", g_end_te2)], incs=[("VE", 1)])
                    NB8 = (NCH + 7) // 8
                    batch_chain = {}
                    batch_chain[-2] = batch_chain[-1] = prev_ch_ve
                    tl8s = {}

                    def tpr_load(j8):
                        nj8 = min(8, NCH - j8 * 8)
                        bp = j8 % 2
                        tl8s[j8] = dma("sync",
                                       tpr[:, bp * 8 * D: bp * 8 * D + nj8 * D]
                                       .rearrange("a (j b) -> a j b", j=nj8),
                                       tp_sp.ap()[h][j8 * 8:j8 * 8 + nj8]
                                       .rearrange("j a b -> a j b"),
                                       waits=[("VE", batch_chain[j8 - 2])],
                                       ch=0)

                    tpr_load(0)
                    if NB8 > 1:
                        tpr_load(1)
                    for j8 in range(NB8):
                        nj8 = min(8, NCH - j8 * 8)
                        bp = j8 % 2
                        for jj in range(nj8):
                            j = j8 * 8 + jj
                            c1 = t.emit("vector", lambda e_, jj=jj, bp=bp: e_.tensor_tensor(
                                stmp[:], s_f[:],
                                tpr[:, bp * 8 * D + jj * D: bp * 8 * D + (jj + 1) * D],
                                OP.add),
                                waits=[("DS", tl8s[j8]), ("VE", ch_v)], incs=[("VE", 1)])
                            c2 = t.emit("vector", lambda e_, h=h, j=j: e_.tensor_scalar(
                                s_f[:], stmp[:], dec[:, h * NCH + j: h * NCH + j + 1],
                                None, OP.mult),
                                waits=[("VE", c1)], incs=[("VE", 1)])
                            ch_v = c2
                            if j < NCH - 1:
                                ch_v = t.emit("vector", lambda e_, j=j: e_.tensor_copy(
                                    sbj[:, (j + 1) * D:(j + 2) * D], s_f[:]),
                                    waits=[("VE", c2)], incs=[("VE", 1)])
                        batch_chain[j8] = ch_v
                        if j8 + 2 < NB8:
                            tpr_load(j8 + 2)
                    CH_VE = ch_v
                    prev_ch_ve = ch_v
                    ASL = [(0, 0), (0, 1), (2, 0), (2, 1)]
                    mv_j = {}
                    for j in range(NCH):
                        pi, hf = ASL[j % 4]
                        w0 = [("VE", vqg)]
                        g = a_guard.get(j % 4)
                        if g:
                            w0.append(g)
                        t.emit("tensor", lambda e_, j=j, pi=pi, hf=hf, off=off: e_.matmul(
                            ps[pi][:, hf * 512: hf * 512 + C],
                            hkt[:, off + j * C: off + (j + 1) * C],
                            hqg[:, off + j * C: off + (j + 1) * C],
                            start=True, stop=True),
                            waits=w0, incs=[("TE", 1)])
                        tev = t.val("TE")
                        mv = t.emit("vector", lambda e_, j=j, pi=pi, hf=hf: e_.tensor_tensor(
                            amb[:, j * C:(j + 1) * C],
                            ps[pi][:, hf * 512: hf * 512 + C],
                            trim[:], OP.mult),
                            waits=[("TE", tev), ("DS", CONST_DS)], incs=[("VE", 1)])
                        a_guard[j % 4] = ("VE", mv)
                        mv_j[j] = mv
                    OSL = [(1, 0), (1, 1), (3, 0), (3, 1)]
                    for j in range(NCH):
                        pi, hf = OSL[j % 4]
                        w1 = [("VE", mv_j[j]), ("DS", VL), ("VE", CH_VE)]
                        g = o_guard.get(j % 4)
                        if g:
                            w1.append(g)
                        t.emit("tensor", lambda e_, j=j, i=i, pi=pi, hf=hf: e_.matmul(
                            ps[pi][:, hf * 512: hf * 512 + D],
                            amb[:, j * C:(j + 1) * C],
                            vgc2[:, j * GHD + i * D: j * GHD + (i + 1) * D],
                            start=True, stop=False),
                            waits=w1, incs=[("TE", 1)])
                        t.emit("tensor", lambda e_, j=j, pi=pi, hf=hf, off=off: e_.matmul(
                            ps[pi][:, hf * 512: hf * 512 + D],
                            hqg[:, off + j * C: off + (j + 1) * C],
                            sbj[:, j * D:(j + 1) * D],
                            start=False, stop=True),
                            waits=[], incs=[("TE", 1)])
                        tev = t.val("TE")
                        oev = t.emit("scalar", lambda e_, j=j, i=i, pi=pi, hf=hf: e_.copy(
                            ost[:, j * GHD + i * D: j * GHD + (i + 1) * D],
                            ps[pi][:, hf * 512: hf * 512 + D]),
                            waits=[("TE", tev), prev_norm], incs=[("SE", 1)])
                        o_guard[j % 4] = ("SE", oev)
                    h_end_te = t.val("TE")
                    hprev[hp] = h_end_te
                O_SE = t.val("SE")
                sq_se = 0
                for j in range(NCH):
                    for i2 in range(GH):
                        sq_se = t.emit("scalar", lambda e_, j=j, i2=i2: e_.activation(
                            mst[:, j * GHD + i2 * D: j * GHD + (i2 + 1) * D],
                            ost[:, j * GHD + i2 * D: j * GHD + (i2 + 1) * D],
                            AF.Square, accum_out=rst[:, j * GH + i2: j * GH + i2 + 1]),
                            waits=[("SE", O_SE), ("VE", prev_norm[1]),
                                   ("DS", prev_mspill)], incs=[("SE", 1)])
                s_ln = t.emit("scalar", lambda e_: e_.activation(
                    rst[:, NCH * GH:2 * NCH * GH], rst[:, 0:NCH * GH], AF.Ln,
                    scale=1.0 / D, bias=cst[:, 3:4]),
                    waits=[("SE", sq_se), ("VE", CST_VE)], incs=[("SE", 1)])
                s_ex = t.emit("scalar", lambda e_: e_.activation(
                    rst[:, 0:NCH * GH], rst[:, NCH * GH:2 * NCH * GH], AF.Exp,
                    scale=-0.5),
                    waits=[("SE", s_ln)], incs=[("SE", 1)])
                s_si = t.emit("scalar", lambda e_: e_.activation(
                    sgb[:], ggc[:], AF.Silu),
                    waits=[("DS", GL), ("SE", s_ex), ("VE", prev_norm[1])],
                    incs=[("SE", 1)])
                ap_v = 0
                for j in range(NCH):
                    for i2 in range(GH):
                        ap_v = t.emit("vector", lambda e_, j=j, i2=i2: e_.tensor_scalar(
                            mst[:, j * GHD + i2 * D: j * GHD + (i2 + 1) * D],
                            ost[:, j * GHD + i2 * D: j * GHD + (i2 + 1) * D],
                            rst[:, j * GH + i2: j * GH + i2 + 1], None, OP.mult),
                            waits=[("SE", s_ex), ("DS", prev_mspill)], incs=[("VE", 1)])
                sil_v = t.emit("vector", lambda e_: e_.tensor_tensor(
                    mst[:], mst[:], sgb[:], OP.mult),
                    waits=[("SE", s_si), ("VE", ap_v)], incs=[("VE", 1)])
                NRM_VE = t.val("VE")
                dst3 = m_sp.ap().rearrange("lt p c -> p lt c")
                for hlf in range(2):
                    prev_mspill = dma("sync",
                                      dst3[:, hlf * (NCH // 2):(hlf + 1) * (NCH // 2),
                                           g0 * GHD:(g0 + 1) * GHD],
                                      mst[:].rearrange("p (j c) -> p j c", j=NCH)
                                      [:, hlf * (NCH // 2):(hlf + 1) * (NCH // 2), :],
                                      waits=[("VE", NRM_VE)])
                prev_norm = ("VE", NRM_VE)
                prev_norm_se = ("SE", t.val("SE"))
                g_end_te2 = t.val("TE")
            t.barrier()
            tapd("m_sp", m_sp, [("DS", t.val("DS"))])


    ln2T = None
    with ExitStack() as phDE:
        eDE = phDE.enter_context
        ln2T = eDE(nc.sbuf_tensor("ln2T", [128, H * T], BF16))

        with ExitStack() as phD:
            eD = phD.enter_context
            wof = eD(nc.sbuf_tensor("wof", [128, H * DM], BF16))
            mlt = eD(nc.sbuf_tensor("mlt", [128, DM], F32))
            mTs = eD(nc.sbuf_tensor("mTs", [128, DM], BF16))
            dsb = eD(nc.sbuf_tensor("dsb", [128, 2 * DM], F32))
            x2s = eD(nc.sbuf_tensor("x2s", [128, DM], F32))
            ln2s = eD(nc.sbuf_tensor("ln2s", [128, DM], F32))

            for kt in range(H):
                wl = dma("gpsimd", wof[:, kt * DM:(kt + 1) * DM],
                         Wo.rearrange("(kt p) c -> kt p c", p=128)[kt])
                t.emit("vector", lambda e_, kt=kt: e_.tensor_scalar(
                    wof[:, kt * DM:(kt + 1) * DM], wof[:, kt * DM:(kt + 1) * DM],
                    nwcol[:, 0:1], None, OP.mult),
                    waits=[("DG", wl), ("DS", CONST_DS)], incs=[("VE", 1)])
            WOF_VE = t.val("VE")

            prev_lt = {"mTs": ("SE", 0), "ps": None, "mlt": ("SE", 0)}
            for lt in range(LT):
                ml = dma("sync", mlt[:], m_sp.ap()[lt], waits=[prev_lt["mTs"]])
                t.wait("tensor", [("DS", ml)])
                for kt in range(H):
                    emit_tr(mlt[:, kt * 128:(kt + 1) * 128],
                            mTs[:, kt * 128:(kt + 1) * 128],
                            [("DS", ml), prev_lt["mTs"]])
                MT_SE = t.val("SE")
                prev_lt["mTs"] = ("SE", MT_SE)
                w0 = [("SE", MT_SE), ("VE", WOF_VE)]
                if prev_lt["ps"]:
                    w0.append(prev_lt["ps"])
                for kt in range(H):
                    for sl in range(DM // 512):
                        t.emit("tensor", lambda e_, kt=kt, sl=sl: e_.matmul(
                            ps[sl // 2][:, (sl % 2) * 512:(sl % 2) * 512 + 512],
                            mTs[:, kt * 128:(kt + 1) * 128],
                            wof[:, kt * DM + sl * 512: kt * DM + (sl + 1) * 512],
                            start=(kt == 0), stop=(kt == H - 1)),
                            waits=w0 if (kt == 0 and sl == 0) else [],
                            incs=[("TE", 1)] if (kt == H - 1 and sl == DM // 512 - 1) else [])
                tev = t.val("TE")
                ev1 = t.emit("vector", lambda e_: e_.tensor_copy(
                    dsb[:, 0:min(1024, DM)], ps[0][:, 0:min(1024, DM)]),
                    waits=[("TE", tev)], incs=[("VE", 1)])
                if DM > 1024:
                    ev1 = t.emit("vector", lambda e_: e_.tensor_copy(
                        dsb[:, 1024:DM], ps[1][:, 0:DM - 1024]),
                        waits=[("TE", tev)], incs=[("VE", 1)])
                prev_lt["ps"] = ("VE", ev1)
                t.wait("vector", [("VE", ev1)])
                t.wait("scalar", [("VE", ev1)])
                l3 = emit_ln(dsb[:, 0:DM], dsb[:, DM:2 * DM], DM, cfg.LNEPS)
                xl = dma("sync", x2s[:], x_t[lt], waits=[("VE", l3)])
                vx = t.emit("vector", lambda e_: e_.tensor_tensor(
                    x2s[:], x2s[:], dsb[:, DM:2 * DM], OP.add),
                    waits=[("DS", xl), ("VE", l3)], incs=[("VE", 1)])
                dma("sync", x2_sp.ap()[lt], x2s[:], waits=[("VE", vx)])
                t.wait("vector", [("VE", vx)])
                t.wait("scalar", [("VE", vx)])
                l2 = emit_ln(x2s[:], ln2s[:], DM, cfg.LNEPS)
                for kt in range(H):
                    emit_tr(ln2s[:, kt * 128:(kt + 1) * 128],
                            ln2T[:, kt * T + lt * 128: kt * T + (lt + 1) * 128],
                            [("VE", l2)])
            LN2T_SE = t.val("SE")
            t.barrier()
            tapd("x2_sp", x2_sp, [("DS", t.val("DS"))])
            tapd("wg_all", wg_all, [("CC2", 1)])
            tapd("wd_all", wd_all, [("CC4", 1)])
            tap("x2", x2s[:], (128, DM), F32, [("VE", t.val("VE"))])
            tap("ln2T", ln2T[:], (128, H * T), BF16, [("SE", LN2T_SE)])

        with ExitStack() as phE:
            eE = phE.enter_context
            hact = eE(nc.sbuf_tensor("hact", [128, OIT_REAL * NHF], BF16))
            wst = eE(nc.sbuf_tensor("wst", [128, 4 * DM], BF16))
            wdsl = eE(nc.sbuf_tensor("wdsl", [128, 2 * DM], BF16))
            sgE = eE(nc.sbuf_tensor("sgE", [128, NHF], BF16))
            odsb = eE(nc.sbuf_tensor("odsb", [128, 4 * DM], F32))
            x2r = eE(nc.sbuf_tensor("x2r", [128, DM], F32))

            wg_t = wg_all.ap()
            wu_t = wu_all.ap()
            wd_t = wd_all.ap()
            NSL = (NHF + 511) // 512

            wslot_guard = {}
            for half in range(NSEC):
                snap = ("VE", t.val("VE"))
                psg = {0: snap, 1: snap, 2: snap, 3: snap}
                prev_vha = 0
                for ot in range(OIT_REAL):
                    pg = [0, 2][ot % 2]
                    pu = [1, 3][ot % 2]
                    sl_w = ot % 2
                    wlg = dma("sync", wst[:, sl_w * 2 * DM: sl_w * 2 * DM + DM],
                              wg_t[ot * 128:(ot + 1) * 128, :],
                              waits=[("CC2", 1), ("TE", wslot_guard.get(sl_w, 0))],
                              ch=sl_w)
                    wlu = dma("scalar", wst[:, sl_w * 2 * DM + DM: sl_w * 2 * DM + 2 * DM],
                              wu_t[ot * 128:(ot + 1) * 128, :],
                              waits=[("CC3", 1), ("TE", wslot_guard.get(sl_w, 0))],
                              ch=sl_w)
                    w0 = [("DS2" if sl_w else "DS", wlg),
                          ("DA2" if sl_w else "DA", wlu), ("SE", LN2T_SE)]
                    if psg.get(pg):
                        w0.append(psg[pg])
                    if psg.get(pu):
                        w0.append(psg[pu])
                    for kt in range(H):
                        for sl in range(NSL):
                            wdt = min(512, NHF - sl * 512)
                            t.emit("tensor", lambda e_, kt=kt, sl=sl, wdt=wdt, pg=pg, half=half, sl_w=sl_w: e_.matmul(
                                ps[pg][:, sl * 512: sl * 512 + wdt],
                                wst[:, sl_w * 2 * DM + kt * 128: sl_w * 2 * DM + (kt + 1) * 128],
                                ln2T[:, kt * T + half * NHF + sl * 512:
                                     kt * T + half * NHF + sl * 512 + wdt],
                                start=(kt == 0), stop=(kt == H - 1)),
                                waits=w0 if (kt == 0 and sl == 0) else [],
                                incs=[("TE", 1)] if (kt == H - 1 and sl == NSL - 1) else [])
                    for kt in range(H):
                        for sl in range(NSL):
                            wdt = min(512, NHF - sl * 512)
                            t.emit("tensor", lambda e_, kt=kt, sl=sl, wdt=wdt, pu=pu, half=half, sl_w=sl_w: e_.matmul(
                                ps[pu][:, sl * 512: sl * 512 + wdt],
                                wst[:, sl_w * 2 * DM + DM + kt * 128: sl_w * 2 * DM + DM + (kt + 1) * 128],
                                ln2T[:, kt * T + half * NHF + sl * 512:
                                     kt * T + half * NHF + sl * 512 + wdt],
                                start=(kt == 0), stop=(kt == H - 1)),
                                waits=[],
                                incs=[("TE", 1)] if (kt == H - 1 and sl == NSL - 1) else [])
                    tev = t.val("TE")
                    wslot_guard[sl_w] = tev
                    ssi = t.emit("scalar", lambda e_, pg=pg: e_.activation(
                        sgE[:, 0:NHF], ps[pg][:, 0:NHF], AF.Silu),
                        waits=[("TE", tev), ("VE", prev_vha)], incs=[("SE", 1)])
                    vha = t.emit("vector", lambda e_, ot=ot, pu=pu: e_.tensor_tensor(
                        hact[:, ot * NHF:(ot + 1) * NHF], sgE[:, 0:NHF],
                        ps[pu][:, 0:NHF], OP.mult),
                        waits=[("SE", ssi)], incs=[("VE", 1)])
                    psg[pg] = ("SE", ssi)
                    psg[pu] = ("VE", vha)
                    prev_vha = vha
                HA_VE = t.val("VE")
                if half == 0:
                    tap("hact", hact[:], (128, OIT_REAL * NHF), BF16,
                        [("VE", HA_VE)])

                NQD = (LT // NSEC + 3) // 4
                NDMH = 2 if DM > 1024 else 1
                DMH = DM // NDMH
                ps_guard = None
                for quad in range(NQD):
                    lts = [half * (LT // NSEC) + quad * 4 + i for i in range(4)
                           if quad * 4 + i < LT // NSEC]
                    for dmh in range(NDMH):
                        wd_guard = {}
                        for ot in range(OIT_REAL):
                            sl_d = ot % 2
                            qn = "sync" if sl_d == 0 else "scalar"
                            sem = ("DS2" if sl_d else "DS") if qn == "sync" else ("DA2" if sl_d else "DA")
                            wl = dma(qn, wdsl[:, sl_d * DM: sl_d * DM + DMH],
                                     wd_t[ot * 128:(ot + 1) * 128,
                                          dmh * DMH:(dmh + 1) * DMH],
                                     waits=[("CC4", 1), ("TE", wd_guard.get(sl_d, 0)),
                                            ("VE", HA_VE), ps_guard], ch=sl_d)
                            w0 = [(sem, wl)]
                            if ps_guard:
                                w0.append(ps_guard)
                            for li, lt in enumerate(lts):
                                ltl = lt - half * (LT // NSEC)
                                for sl in range(DMH // 512):
                                    t.emit("tensor", lambda e_, ot=ot, sl=sl, ltl=ltl, li=li, sl_d=sl_d: e_.matmul(
                                        ps[(li * (DMH // 512) + sl) // 2][:, ((li * (DMH // 512) + sl) % 2) * 512:((li * (DMH // 512) + sl) % 2) * 512 + 512],
                                        hact[:, ot * NHF + ltl * 128: ot * NHF + (ltl + 1) * 128],
                                        wdsl[:, sl_d * DM + sl * 512: sl_d * DM + (sl + 1) * 512],
                                        start=(ot == 0), stop=(ot == OIT_REAL - 1)),
                                        waits=w0 if (sl == 0 and li == 0) else [],
                                        incs=[("TE", 1)] if (sl == DMH // 512 - 1 and li == len(lts) - 1) else [])
                            wd_guard[sl_d] = t.val("TE")
                        tev = t.val("TE")
                        for li, lt in enumerate(lts):
                            base = li * (DMH // 512)
                            ev1 = t.emit("vector", lambda e_, li=li, dmh=dmh, base=base: e_.tensor_copy(
                                odsb[:, li * DM + dmh * DMH: li * DM + dmh * DMH + DMH],
                                ps[base // 2][:, (base % 2) * 512:(base % 2) * 512 + DMH]),
                                waits=[("TE", tev)], incs=[("VE", 1)])
                        ps_guard = ("VE", t.val("VE"))
                    for li, lt in enumerate(lts):
                        t.wait("vector", [ps_guard])
                        t.wait("scalar", [ps_guard])
                        l4 = emit_ln(odsb[:, li * DM:(li + 1) * DM],
                                     odsb[:, li * DM:(li + 1) * DM], DM, cfg.LNEPS)
                        xl = dma("sync", x2r[:], x2_sp.ap()[lt], waits=[("VE", l4)])
                        vo = t.emit("vector", lambda e_, li=li: e_.tensor_tensor(
                            x2r[:], x2r[:], odsb[:, li * DM:(li + 1) * DM], OP.add),
                            waits=[("DS", xl), ("VE", l4)], incs=[("VE", 1)])
                        dma("sync", out_t[lt], x2r[:], waits=[("VE", vo)])
                    ps_guard = ("VE", t.val("VE"))
            t.wait("sync", [("DS", t.val("DS")), ("DA", t.val("DA")),
                            ("DG", t.val("DG"))])

    t.finalize(block)
    ctx.close()
    return nc, dbg



def _consts(cfg, core):
    C = cfg.C
    ident = np.eye(128, dtype=np.float32)
    r = np.arange(C)
    u64 = np.where(r[:, None] <= r[None, :], 1.0 / 16.0, 0.0).astype(ml_dtypes.bfloat16)
    trimm = np.where(r[:, None] <= r[None, :], 1.0, 0.0).astype(ml_dtypes.bfloat16)
    fmv = np.zeros((128, 8), np.float32)
    seq, span = core // 4, core % 4
    for rk in range(8):
        if rk // 4 == seq and rk % 4 < span:
            fmv[:, rk] = 1.0
    return ident, u64, trimm, fmv, 1.0 - fmv


def kernel(L0_attn_x, L0_attn_Wq, L0_attn_Wk, L0_attn_Wv, L0_attn_Wg1,
           L0_attn_Wg2, L0_attn_bg2, L0_attn_Wgate, L0_attn_norm_w, L0_attn_Wo,
           L0_ffn_lut, L0_ffn_gate_idx, L0_ffn_up_idx, L0_ffn_down_idx):
    cfg = Cfg()
    nc, _ = build_core(cfg)

    x = np.ascontiguousarray(np.asarray(L0_attn_x, np.float32).reshape(2 * 8192, 2048))
    lut = np.asarray(L0_ffn_lut, np.float32).reshape(16)
    lutb = np.tile(lut[None, :], (128, 1)).astype(np.float32)
    gi = np.asarray(L0_ffn_gate_idx, np.int32)
    ui = np.asarray(L0_ffn_up_idx, np.int32)
    di = np.asarray(L0_ffn_down_idx, np.int32)
    INTER = cfg.INTER
    PADR = cfg.OIT_PC * 128 * 8

    def pad_rows(a):
        return np.concatenate(
            [a, np.zeros((PADR - a.shape[0], a.shape[1]), np.int32)], axis=0)

    gi_p = pad_rows(gi)
    ui_p = pad_rows(ui)
    diT = np.ascontiguousarray(di.T)
    diT_p = pad_rows(diT)

    shared = {
        "Wq": np.asarray(L0_attn_Wq, np.float32),
        "Wk": np.asarray(L0_attn_Wk, np.float32),
        "Wv": np.asarray(L0_attn_Wv, np.float32),
        "Wg1": np.asarray(L0_attn_Wg1, np.float32),
        "Wg2b": np.concatenate([np.asarray(L0_attn_Wg2, np.float32),
                                np.asarray(L0_attn_bg2, np.float32).reshape(1, 2048)], 0),
        "Wgate": np.asarray(L0_attn_Wgate, np.float32),
        "norm_w": np.asarray(L0_attn_norm_w, np.float32).reshape(128, 1),
        "Wo": np.asarray(L0_attn_Wo, np.float32),
        "lutb": lutb,
    }
    in_maps = []
    NR = cfg.OIT_PC * 128
    for c in range(8):
        ident, u64, trimm, fmv, fmiv = _consts(cfg, c)
        seq, span = c // 4, c % 4
        sl = slice(seq * 8192 + span * 2048, seq * 8192 + (span + 1) * 2048)
        m = dict(shared)
        m.update({
            "x": np.ascontiguousarray(x[sl]),
            "gidx": np.ascontiguousarray(gi_p[c * NR:(c + 1) * NR]),
            "uidx": np.ascontiguousarray(ui_p[c * NR:(c + 1) * NR]),
            "didxT": np.ascontiguousarray(diT_p[c * NR:(c + 1) * NR]),
            "identc": ident, "u64c": u64, "trimc": trimm,
            "fmc": fmv, "fmic": fmiv,
        })
        in_maps.append(m)

    res = run_bass_kernel_spmd(nc, in_maps, core_ids=list(range(8)))
    outp = np.empty((2, 8192, 2048), np.float32)
    for c in range(8):
        seq, span = c // 4, c % 4
        outp[seq, span * 2048:(span + 1) * 2048] = np.asarray(res.results[c]["out"])
    return outp


# version 19
# speedup vs baseline: 1.0178x; 1.0178x over previous
import numpy as np
import ml_dtypes
from contextlib import ExitStack

import concourse.bass as bass
import concourse.mybir as mybir
from concourse.bass_utils import run_bass_kernel_spmd
from concourse.alu_op_type import AluOpType as OP

F32 = mybir.dt.float32
BF16 = mybir.dt.bfloat16
I32 = mybir.dt.int32
AF = mybir.ActivationFunctionType
AX = mybir.AxisListType

ENGINES = ("tensor", "vector", "scalar", "gpsimd", "sync")


class Cfg:
    def __init__(self, H=16, LT=16, OIT_REAL=43, OIT_PC=6, taps=()):
        self.H = H
        self.D = 128
        self.C = 128
        self.LT = LT
        self.T = LT * 128
        self.NCH = LT * 128 // 128
        self.DM = H * 128
        self.GH = 2
        self.OIT_REAL = OIT_REAL
        self.OIT_PC = OIT_PC
        self.INTER = OIT_REAL * 128
        self.LOWR = 16
        self.taps = set(taps)
        self.LNEPS = 1e-5
        self.RMSEPS = 1e-5
        assert self.OIT_PC * 8 >= self.OIT_REAL
        assert self.H % self.GH == 0 and self.T % 512 == 0
        assert self.GH * 128 <= 512


class Tape:
    def __init__(self, nc):
        self.nc = nc
        self.ops = {e: [] for e in ENGINES}
        self.h = {}
        self.count = {}
        self.seen = {e: {} for e in ENGINES}

    def add_sem(self, name, handle):
        self.h[name] = handle
        self.count[name] = 0

    def val(self, name):
        return self.count[name]

    def emit(self, engine, fn, waits=(), incs=()):
        wl = []
        for w in waits:
            if w is None:
                continue
            (s, v) = w
            if v is None or v <= 0 or self.seen[engine].get(s, 0) >= v:
                continue
            self.seen[engine][s] = v
            wl.append((s, v))
        for (s, n) in incs:
            self.count[s] += n
        h = self.h
        wl, incs = tuple(wl), tuple(incs)

        def item(eng, fn=fn, wl=wl, incs=incs):
            for (s, v) in wl:
                eng.wait_ge(h[s], v)
            instr = fn(eng)
            for (s, n) in incs:
                assert instr is not None
                instr = instr.then_inc(h[s], n)

        self.ops[engine].append(item)
        return self.count[incs[0][0]] if incs else None

    def wait(self, engine, waits):
        self.emit(engine, lambda e_: None, waits=waits)

    def barrier(self, sems=("TE", "VE", "SE", "DS", "DA", "DG", "DS2", "DA2", "DG2")):
        snap = [(s, self.count[s]) for s in sems]
        for e in ENGINES:
            self.wait(e, snap)

    def finalize(self, block):
        for ename in ENGINES:
            def body(eng, _ops=self.ops[ename]):
                for it in _ops:
                    it(eng)
            getattr(block, ename)(body)


def build_core(cfg: Cfg):
    nc = bass.Bass(target_bir_lowering=False)
    H, D, C, LT, T, DM, GH = cfg.H, cfg.D, cfg.C, cfg.LT, cfg.T, cfg.DM, cfg.GH
    NCH, NG, LOWR = cfg.NCH, cfg.H // cfg.GH, cfg.LOWR
    OIT_PC, OIT_REAL = cfg.OIT_PC, cfg.OIT_REAL
    GHD = GH * D
    NHF = T // 4
    NSEC = 4
    WIN = 1024 if T % 1024 == 0 else 512
    WIN2 = 1024 if DM % 1024 == 0 else 512
    LNS = float(-0.5 * np.log(float(D)))

    x_in = nc.dram_tensor("x", [T, DM], F32, kind="ExternalInput").ap()
    Wq = nc.dram_tensor("Wq", [DM, DM], F32, kind="ExternalInput").ap()
    Wk = nc.dram_tensor("Wk", [DM, DM], F32, kind="ExternalInput").ap()
    Wv = nc.dram_tensor("Wv", [DM, DM], F32, kind="ExternalInput").ap()
    Wg1 = nc.dram_tensor("Wg1", [DM, LOWR], F32, kind="ExternalInput").ap()
    Wg2b = nc.dram_tensor("Wg2b", [LOWR + 1, DM], F32, kind="ExternalInput").ap()
    Wgate = nc.dram_tensor("Wgate", [DM, DM], F32, kind="ExternalInput").ap()
    norm_w = nc.dram_tensor("norm_w", [D, 1], F32, kind="ExternalInput").ap()
    Wo = nc.dram_tensor("Wo", [DM, DM], F32, kind="ExternalInput").ap()
    gidx = nc.dram_tensor("gidx", [OIT_PC * 128, DM], I32, kind="ExternalInput").ap()
    uidx = nc.dram_tensor("uidx", [OIT_PC * 128, DM], I32, kind="ExternalInput").ap()
    didxT = nc.dram_tensor("didxT", [OIT_PC * 128, DM], I32, kind="ExternalInput").ap()
    lutb = nc.dram_tensor("lutb", [128, 16], F32, kind="ExternalInput").ap()
    identc = nc.dram_tensor("identc", [128, 128], F32, kind="ExternalInput").ap()
    u64c = nc.dram_tensor("u64c", [128, C], BF16, kind="ExternalInput").ap()
    trimc = nc.dram_tensor("trimc", [128, C], BF16, kind="ExternalInput").ap()
    fmc = nc.dram_tensor("fmc", [128, 8], F32, kind="ExternalInput").ap()
    fmic = nc.dram_tensor("fmic", [128, 8], F32, kind="ExternalInput").ap()

    out = nc.dram_tensor("out", [T, DM], F32, kind="ExternalOutput").ap()

    q_sp = nc.dram_tensor("q_sp", [H, D, T], BF16, kind="Internal")
    k_sp = nc.dram_tensor("k_sp", [H, D, T], BF16, kind="Internal")
    e1_sp = nc.dram_tensor("e1_sp", [H, D, T], BF16, kind="Internal")
    kt_sp = nc.dram_tensor("kt_sp", [H, D, T], BF16, kind="Internal")
    v_sp = nc.dram_tensor("v_sp", [LT, 128, DM], BF16, kind="Internal")
    g_sp = nc.dram_tensor("g_sp", [LT, 128, DM], BF16, kind="Internal")
    tp_sp = nc.dram_tensor("tp_sp", [H, NCH, D, D], F32, kind="Internal")
    m_sp = nc.dram_tensor("m_sp", [LT, 128, DM], F32, kind="Internal")
    x2_sp = nc.dram_tensor("x2_sp", [LT, 128, DM], F32, kind="Internal")
    td_in = nc.dram_tensor("td_in", [128, H * (D + 1)], F32, kind="Internal")
    td_out = nc.dram_tensor("td_out", [8 * 128, H * (D + 1)], F32, kind="Internal")
    wg_in = nc.dram_tensor("wg_in", [OIT_PC * 128, DM], BF16, kind="Internal")
    wu_in = nc.dram_tensor("wu_in", [OIT_PC * 128, DM], BF16, kind="Internal")
    wd_in = nc.dram_tensor("wd_in", [OIT_PC * 128, DM], BF16, kind="Internal")
    wg_all = nc.dram_tensor("wg_all", [8 * OIT_PC * 128, DM], BF16, kind="Internal")
    wu_all = nc.dram_tensor("wu_all", [8 * OIT_PC * 128, DM], BF16, kind="Internal")
    wd_all = nc.dram_tensor("wd_all", [8 * OIT_PC * 128, DM], BF16, kind="Internal")

    x_t = x_in.rearrange("(lt p) d -> lt p d", p=128)
    out_t = out.rearrange("(lt p) d -> lt p d", p=128)

    ctx = ExitStack()
    e = ctx.enter_context
    t = Tape(nc)
    for s in ("TE", "VE", "SE", "GP", "DS", "DA", "DG", "DS2", "DA2", "DG2", "CC1", "CC2", "CC3", "CC4"):
        t.add_sem(s, e(nc.semaphore(s)))

    dbg = {}

    def tap(name, buf, shape, dtype, waits):
        if name not in cfg.taps:
            return
        ap = nc.dram_tensor("dbg_" + name, list(shape), dtype, kind="ExternalOutput").ap()
        t.emit("gpsimd", lambda e_: e_.dma_start(out=ap, in_=buf), waits=waits,
               incs=[("DG", 16)])
        dbg[name] = True

    def tapd(name, dram, waits):
        if name not in cfg.taps:
            return
        shape = list(dram.shape)
        ap = nc.dram_tensor("dbg_" + name, shape, dram.dtype, kind="ExternalOutput").ap()
        t.emit("gpsimd", lambda e_: e_.dma_start(out=ap, in_=dram.ap()), waits=waits,
               incs=[("DG", 16)])
        dbg[name] = True

    ident = e(nc.sbuf_tensor("ident", [128, 128], F32))
    identb = e(nc.sbuf_tensor("identb", [128, 128], BF16))
    u64 = e(nc.sbuf_tensor("u64", [128, C], BF16))
    trim = e(nc.sbuf_tensor("trim", [128, C], BF16))
    fm = e(nc.sbuf_tensor("fm", [128, 8], F32))
    fmi = e(nc.sbuf_tensor("fmi", [128, 8], F32))
    nwcol = e(nc.sbuf_tensor("nwcol", [128, 1], F32))
    lut_s = e(nc.sbuf_tensor("lut_s", [128, 16], F32))
    wg2a = e(nc.sbuf_tensor("wg2a", [32, DM], BF16))
    yTb = e(nc.sbuf_tensor("yTb", [32, T], BF16))
    stat = e(nc.sbuf_tensor("stat", [128, 16], F32))
    trash = e(nc.sbuf_tensor("trash", [128, DM], BF16))
    s_f = e(nc.sbuf_tensor("s_f", [128, D], F32))
    stmp = e(nc.sbuf_tensor("stmp", [128, D], F32))
    cst = e(nc.sbuf_tensor("cst", [128, 8], F32))

    ps = [e(nc.psum_tensor(f"ps{i}", [128, 1024], F32)) for i in range(4)]

    block = e(nc.Block())

    def dma(queue, dst, src, waits=(), ch=0, **kw):
        sem = {"sync": "DS", "scalar": "DA", "gpsimd": "DG"}[queue]
        if ch:
            sem += "2"
        w = list(waits) + [(sem, t.val(sem))]
        return t.emit(queue, lambda e_: e_.dma_start(out=dst, in_=src, **kw),
                      waits=w, incs=[(sem, 16)])

    dma("sync", ident[:], identc)
    dma("sync", u64[:], u64c)
    dma("sync", trim[:], trimc)
    dma("sync", fm[:], fmc)
    dma("sync", fmi[:], fmic)
    dma("sync", nwcol[:], norm_w)
    CONST_DS = dma("sync", lut_s[:], lutb)
    CONST_DG = dma("gpsimd", wg2a[0:LOWR + 1, :], Wg2b)
    t.emit("vector", lambda e_: e_.tensor_copy(identb[:], ident[:]),
           waits=[("DS", CONST_DS)], incs=[("VE", 1)])
    for ci, cval in ((0, 1e-5), (1, 1.0), (2, float(-0.5 * np.log(128.0))), (3, 1e-5)):
        t.emit("vector", lambda e_, ci=ci, cval=cval: e_.memset(cst[:, ci:ci + 1], cval),
               incs=[("VE", 1)])
    CST_VE = t.val("VE")

    def emit_ln(src_ap, dst_ap, width, eps):
        v1 = t.emit("vector", lambda e_: e_.tensor_reduce(
            stat[:, 0:1], src_ap, AX.X, OP.add), incs=[("VE", 1)])
        s1 = t.emit("scalar", lambda e_: e_.activation(
            trash[:, 0:width], src_ap, AF.Square, accum_out=stat[:, 1:2]),
            incs=[("SE", 1)])
        v2 = t.emit("vector", lambda e_: e_.tensor_scalar(
            stat[:, 2:3], stat[:, 0:1], 1.0 / width, None, OP.mult),
            waits=[("VE", v1), ("SE", s1)], incs=[("VE", 1)])
        v3 = t.emit("vector", lambda e_: e_.tensor_scalar(
            stat[:, 3:4], stat[:, 1:2], 1.0 / width, None, OP.mult),
            waits=[("VE", v2)], incs=[("VE", 1)])
        v4 = t.emit("vector", lambda e_: e_.tensor_tensor(
            stat[:, 4:5], stat[:, 2:3], stat[:, 2:3], OP.mult),
            waits=[("VE", v3)], incs=[("VE", 1)])
        v5 = t.emit("vector", lambda e_: e_.tensor_tensor(
            stat[:, 5:6], stat[:, 3:4], stat[:, 4:5], OP.subtract),
            waits=[("VE", v4)], incs=[("VE", 1)])
        s2 = t.emit("scalar", lambda e_: e_.activation(
            stat[:, 6:7], stat[:, 5:6], AF.Ln, bias=cst[:, 0:1]),
            waits=[("VE", v5), ("VE", CST_VE)], incs=[("SE", 1)])
        s3 = t.emit("scalar", lambda e_: e_.activation(
            stat[:, 7:8], stat[:, 6:7], AF.Exp, scale=-0.5),
            waits=[("SE", s2)], incs=[("SE", 1)])
        v6 = t.emit("vector", lambda e_: e_.tensor_tensor(
            stat[:, 8:9], stat[:, 2:3], stat[:, 7:8], OP.mult),
            waits=[("SE", s3)], incs=[("VE", 1)])
        v7 = t.emit("vector", lambda e_: e_.tensor_scalar(
            stat[:, 9:10], stat[:, 8:9], -1.0, None, OP.mult),
            waits=[("VE", v6)], incs=[("VE", 1)])
        v8 = t.emit("vector", lambda e_: e_.tensor_scalar(
            dst_ap, src_ap, stat[:, 7:8], stat[:, 9:10], OP.mult, OP.add),
            waits=[("VE", v7)], incs=[("VE", 1)])
        return v8

    tr_guard = {}
    tr_i = [0]

    def emit_tr(src_ap, dst_ap, waits, out_rows=128, extra_evac_waits=()):
        half = tr_i[0] % 2
        tr_i[0] += 1
        pslot = ps[3][0:out_rows, half * 512: half * 512 + 128]
        w = list(waits)
        g = tr_guard.get(half)
        if g:
            w.append(g)
        t.emit("tensor", lambda e_: e_.transpose(pslot, src_ap, ident[:]),
               waits=w, incs=[("TE", 1)])
        tev = t.val("TE")
        ev = t.emit("scalar", lambda e_: e_.copy(dst_ap, pslot),
                    waits=[("TE", tev)] + list(extra_evac_waits), incs=[("SE", 1)])
        tr_guard[half] = ("SE", ev)
        return ev

    with ExitStack() as phAttn:
        eAt = phAttn.enter_context
        dec = eAt(nc.sbuf_tensor("dec", [128, H * NCH], F32))
        sin_f = eAt(nc.sbuf_tensor("sin_f", [128, H * D], F32))
        mat_i = eAt(nc.sbuf_tensor("mat_i", [128, DM], BF16))
        mat_a = eAt(nc.sbuf_tensor("mat_a", [128, DM], BF16))
        mat_k = eAt(nc.sbuf_tensor("mat_k", [128, DM], BF16))
        mat_s = eAt(nc.sbuf_tensor("mat_s", [128, DM], BF16))
        mtmp = eAt(nc.sbuf_tensor("mtmp", [128, 128], F32))
        mprev = {"store": 0, "chain": 0, "tr_se": 0}
        MQ = [(gidx, wg_in, True, ot) for ot in range(OIT_PC)] + \
             [(uidx, wu_in, True, ot) for ot in range(OIT_PC)] + \
             [(didxT, wd_in, False, ot) for ot in range(OIT_PC)]

        GATHER_OF = [(wg_in, wg_all, "CC2"), (wu_in, wu_all, "CC3"),
                     (wd_in, wd_all, "CC4")]

        def _m_generator():
            for idx_ap, in_dram, transpose, ot in MQ:
                ld = dma("gpsimd", mat_i[:], idx_ap[ot * 128:(ot + 1) * 128, :],
                         waits=[("VE", mprev["chain"])], ch=ot % 2)
                sem_ld = "DG2" if ot % 2 else "DG"
                va = t.emit("vector", lambda e_: e_.tensor_scalar(
                    mat_a[:], mat_i[:], 0.0, lut_s[:, 0:1], OP.is_equal, OP.mult),
                    waits=[(sem_ld, ld), ("DS", CONST_DS), ("DS", mprev["store"]),
                           ("DA", mprev["store"]), ("SE", mprev["tr_se"])],
                    incs=[("VE", 1)])
                yield
                for jj in range(1, 16):
                    vk = t.emit("vector", lambda e_, jj=jj: e_.tensor_scalar(
                        mat_k[:], mat_i[:], float(jj), lut_s[:, jj:jj + 1],
                        OP.is_equal, OP.mult),
                        waits=[("VE", va)], incs=[("VE", 1)])
                    va = t.emit("vector", lambda e_: e_.tensor_tensor(
                        mat_a[:], mat_a[:], mat_k[:], OP.add),
                        waits=[("VE", vk)], incs=[("VE", 1)])
                    yield
                mprev["chain"] = va
                if not transpose:
                    mprev["store"] = dma("scalar", in_dram.ap()[ot * 128:(ot + 1) * 128, :],
                                         mat_a[:], waits=[("VE", va)])
                    yield
                else:
                    for kt in range(H):
                        vv = t.emit("vector", lambda e_, kt=kt: e_.tensor_copy(
                            mtmp[:], mat_a[:, kt * 128:(kt + 1) * 128]),
                            waits=[("VE", va), ("TE", t.val("TE"))],
                            incs=[("VE", 1)])
                        emit_tr(mtmp[:], mat_s[:, kt * 128:(kt + 1) * 128],
                                [("VE", vv)], extra_evac_waits=[("DA", mprev["store"]),
                                                                ("DS", mprev["store"])])
                        if kt % 4 == 3:
                            yield
                    mprev["tr_se"] = t.val("SE")
                    mprev["chain"] = t.val("VE")
                    mprev["store"] = dma("scalar", in_dram.ap()[ot * 128:(ot + 1) * 128, :],
                                         mat_s[:], waits=[("SE", mprev["tr_se"])])
                    yield
                if ot == OIT_PC - 1:
                    for (g_in, g_all, g_sem) in GATHER_OF:
                        if g_in is in_dram:
                            t.emit("gpsimd", lambda e_, g_in=g_in, g_all=g_all:
                                   e_.collective_compute(
                                       "AllGather", OP.bypass,
                                       replica_groups=[list(range(8))],
                                       ins=[g_in.ap()], outs=[g_all.ap()]),
                                   waits=[("DA", mprev["store"]),
                                          ("DS", mprev["store"])],
                                   incs=[(g_sem, 1)])
                    yield
            while True:
                yield

        _MG = _m_generator()

        def emit_mtile(n=3):
            for _ in range(n):
                next(_MG)

        ln1T_scope = ExitStack()
        ln1T = ln1T_scope.enter_context(nc.sbuf_tensor("ln1T", [128, H * T], BF16))

        with ExitStack() as ph0:
            e0 = ph0.enter_context
            xslab = e0(nc.sbuf_tensor("xslab", [128, 2 * DM], F32))
            ln1s = e0(nc.sbuf_tensor("ln1s", [128, 2 * DM], F32))
            wslab = e0(nc.sbuf_tensor("wslab", [128, H * WIN2], BF16))
            wg1s = e0(nc.sbuf_tensor("wg1s", [128, H * LOWR], BF16))
            yT = e0(nc.sbuf_tensor("yT", [32, T], F32))
            evb = e0(nc.sbuf_tensor("evb", [128, 2 * DM], BF16))

            apply_v = {}
            tr_after = {}
            for lt in range(LT):
                par = lt % 2
                xb = xslab[:, par * DM:(par + 1) * DM]
                w = [("DS", CONST_DS)]
                if lt >= 2:
                    w.append(("VE", apply_v[lt - 2]))
                xl = dma("sync", xb, x_t[lt], waits=w)
                t.wait("vector", [("DS", xl)])
                t.wait("scalar", [("DS", xl)])
                if lt >= 2:
                    t.wait("vector", [("TE", tr_after[lt - 2])])
                lv = emit_ln(xb, ln1s[:, par * DM:(par + 1) * DM], DM, cfg.LNEPS)
                apply_v[lt] = lv
                for kt in range(H):
                    emit_tr(ln1s[:, par * DM + kt * 128: par * DM + (kt + 1) * 128],
                            ln1T[:, kt * T + lt * 128: kt * T + (lt + 1) * 128],
                            [("VE", lv)])
                tr_after[lt] = t.val("TE")
            LN1T_SE = t.val("SE")
            tap("ln1T", ln1T[:], (128, H * T), BF16, [("SE", LN1T_SE)])

            dma("gpsimd", wg1s[:].rearrange("p (kt r) -> p kt r", kt=H),
                Wg1.rearrange("(kt p) r -> p kt r", p=128))
            G_LD = t.val("DG")
            yg = {}
            for nq in range(T // 512):
                half = nq % 2
                w0 = [("DG", G_LD), ("SE", LN1T_SE)]
                if nq >= 2:
                    w0.append(yg[nq - 2])
                for kt in range(H):
                    t.emit("tensor", lambda e_, kt=kt, nq=nq, half=half: e_.matmul(
                        ps[0][0:LOWR, half * 512:half * 512 + 512],
                        wg1s[:, kt * LOWR:(kt + 1) * LOWR],
                        ln1T[:, kt * T + nq * 512: kt * T + (nq + 1) * 512],
                        start=(kt == 0), stop=(kt == H - 1)),
                        waits=w0 if kt == 0 else [],
                        incs=[("TE", 1)] if kt == H - 1 else [])
                tev = t.val("TE")
                ev = t.emit("scalar", lambda e_, nq=nq, half=half: e_.copy(
                    yT[0:LOWR, nq * 512:(nq + 1) * 512],
                    ps[0][0:LOWR, half * 512:half * 512 + 512]),
                    waits=[("TE", tev)], incs=[("SE", 1)])
                yg[nq] = ("SE", ev)
            vms = t.emit("vector", lambda e_: e_.memset(yTb[0:32, :], 1.0),
                         incs=[("VE", 1)])
            YT_VE = t.emit("vector", lambda e_: e_.tensor_copy(
                yTb[0:LOWR, :], yT[0:LOWR, :]),
                waits=[("SE", t.val("SE")), ("VE", vms)], incs=[("VE", 1)])
            tap("yT", yT[0:LOWR, :], (LOWR, T), F32, [("SE", t.val("SE"))])

            def proj_tm(W_ap, spill):
                NW = DM // WIN2
                NHN = WIN2 // 512
                ev_guard = {0: None, 1: None}
                sp_guard = {0: None, 1: None}
                entry = [("SE", t.val("SE")), ("VE", t.val("VE")), ("DS", t.val("DS"))]
                for win in range(NW):
                    wl = dma("gpsimd", wslab[:, 0:H * WIN2].rearrange("p (kt c) -> p kt c", kt=H),
                             W_ap.rearrange("(kt p) c -> p kt c", p=128)[:, :, win * WIN2:(win + 1) * WIN2],
                             waits=[("TE", t.val("TE"))])
                    for lt in range(LT):
                        pb = lt % 2
                        w0 = [("DG", wl), ("SE", LN1T_SE)] + entry
                        if ev_guard[pb]:
                            w0.append(ev_guard[pb])
                        for kt in range(H):
                            for hn in range(NHN):
                                t.emit("tensor", lambda e_, kt=kt, lt=lt, hn=hn, pb=pb: e_.matmul(
                                    ps[pb][:, hn * 512:(hn + 1) * 512],
                                    ln1T[:, kt * T + lt * 128: kt * T + (lt + 1) * 128],
                                    wslab[:, kt * WIN2 + hn * 512: kt * WIN2 + (hn + 1) * 512],
                                    start=(kt == 0), stop=(kt == H - 1)),
                                    waits=w0 if (kt == 0 and hn == 0) else [],
                                    incs=[("TE", 1)] if (kt == H - 1 and hn == NHN - 1) else [])
                        tev = t.val("TE")
                        evw = [("TE", tev)] + entry
                        if sp_guard[pb]:
                            evw.append(sp_guard[pb])
                        evv = t.emit("vector", lambda e_, lt=lt, win=win, pb=pb: e_.tensor_copy(
                            evb[:, pb * DM + win * WIN2: pb * DM + (win + 1) * WIN2],
                            ps[pb][:, 0:WIN2]),
                            waits=evw, incs=[("VE", 1)])
                        ev_guard[pb] = ("VE", evv)
                        sd = dma("sync",
                                 spill.ap()[lt][:, win * WIN2:(win + 1) * WIN2],
                                 evb[:, pb * DM + win * WIN2: pb * DM + (win + 1) * WIN2],
                                 waits=[("VE", evv)])
                        sp_guard[pb] = ("DS", sd)
                return t.val("DS")

            proj_tm(Wv, v_sp)
            proj_tm(Wgate, g_sp)
            t.barrier()
            tapd("v_sp", v_sp, [("DS", t.val("DS"))])
            tapd("g_sp", g_sp, [("DS", t.val("DS"))])

        def load_chunk_major(dst, spill, g0, waits):
            last = None
            src3 = spill.ap().rearrange("lt p c -> p lt c")
            for hlf in range(2):
                last = dma("sync",
                           dst[:].rearrange("p (j c) -> p j c", j=NCH)
                           [:, hlf * (NCH // 2):(hlf + 1) * (NCH // 2), :],
                           src3[:, hlf * (NCH // 2):(hlf + 1) * (NCH // 2),
                                g0 * GHD:(g0 + 1) * GHD],
                           waits=waits)
            return last

        with ExitStack() as phA0:
            eA0 = phA0.enter_context
            wfull = eA0(nc.sbuf_tensor("wfull", [128, H * DM], BF16))
            pstg = eA0(nc.sbuf_tensor("pstg", [128, 2 * WIN], BF16))
            NHN0 = WIN // 512
            A0_entry = [("SE", t.val("SE")), ("VE", t.val("VE"))]
            pj_guard = {0: None, 1: None}
            st_guard = {0: None, 1: None}
            for (Wap, spill) in ((Wq, q_sp), (Wk, k_sp)):
                wprev_te = t.val("TE")
                wl = 0
                for kb in range(4):
                    wl = dma("gpsimd",
                             wfull[:, kb * (H // 4) * DM:(kb + 1) * (H // 4) * DM]
                             .rearrange("p (kt c) -> p kt c", kt=H // 4),
                             Wap.rearrange("(kt p) c -> p kt c", p=128)
                             [:, kb * (H // 4):(kb + 1) * (H // 4), :],
                             waits=[("TE", wprev_te)], ch=kb % 2)
                wl = t.val("DG")
                wl2 = t.val("DG2")
                for mt in range(H):
                    for win in range(T // WIN):
                        pb = (mt * (T // WIN) + win) % 2
                        w0 = [("DG", wl), ("DG2", wl2), ("SE", LN1T_SE)] + A0_entry
                        if pj_guard[pb]:
                            w0.append(pj_guard[pb])
                        for kt in range(H):
                            for hn in range(NHN0):
                                t.emit("tensor", lambda e_, kt=kt, mt=mt, win=win, hn=hn, pb=pb: e_.matmul(
                                    ps[pb][:, hn * 512:(hn + 1) * 512],
                                    wfull[:, kt * DM + mt * 128: kt * DM + (mt + 1) * 128],
                                    ln1T[:, kt * T + win * WIN + hn * 512:
                                         kt * T + win * WIN + (hn + 1) * 512],
                                    start=(kt == 0), stop=(kt == H - 1)),
                                    waits=w0 if (kt == 0 and hn == 0) else [],
                                    incs=[("TE", 1)] if (kt == H - 1 and hn == NHN0 - 1) else [])
                        tev = t.val("TE")
                        evw = [("TE", tev)]
                        if st_guard[pb]:
                            evw.append(st_guard[pb])
                        evv = t.emit("vector", lambda e_, pb=pb: e_.tensor_copy(
                            pstg[:, pb * WIN:(pb + 1) * WIN], ps[pb][:, 0:WIN]),
                            waits=evw, incs=[("VE", 1)])
                        pj_guard[pb] = ("VE", evv)
                        sd = dma("sync", spill.ap()[mt][:, win * WIN:(win + 1) * WIN],
                                 pstg[:, pb * WIN:(pb + 1) * WIN],
                                 waits=[("VE", evv)], ch=pb)
                        st_guard[pb] = ("DS2" if pb else "DS", sd)
                        emit_mtile(4)
            emit_mtile(500)
            t.barrier()

        with ExitStack() as phA:
            eA = phA.enter_context
            tdst = eA(nc.sbuf_tensor("tdst", [128, H * (D + 1)], F32))
            hk = eA(nc.sbuf_tensor("hk", [128, T], BF16))
            Gg = eA(nc.sbuf_tensor("Gg", [128, GH * T], F32))
            sps = eA(nc.sbuf_tensor("sps", [128, LT * GH * 128], BF16))
            spf = eA(nc.sbuf_tensor("spf", [128, 2 * GH * 128], F32))
            e1b = eA(nc.sbuf_tensor("e1b", [128, GH * T], BF16))
            e6b = eA(nc.sbuf_tensor("e6b", [128, GH * T], BF16))
            ktb = eA(nc.sbuf_tensor("ktb", [128, T], BF16))
            ktm = eA(nc.sbuf_tensor("ktm", [128, NCH * D], BF16))
            ktm2 = eA(nc.sbuf_tensor("ktm2", [128, NCH * D], BF16))
            ktb2 = eA(nc.sbuf_tensor("ktb2", [128, T], F32))
            ktf = eA(nc.sbuf_tensor("ktf", [128, T], F32))
            glcA = eA(nc.sbuf_tensor("glcA", [128, GH * 32], F32))
            glcB = eA(nc.sbuf_tensor("glcB", [128, GH * 32], F32))
            sdecS = eA(nc.sbuf_tensor("sdecS", [128, GH * 32], F32))
            vgc = eA(nc.sbuf_tensor("vgc", [128, NCH * GHD], BF16))
            tps = eA(nc.sbuf_tensor("tps", [128, 8 * D], F32))

            prev_sp = {"e1": 0, "kt": 0, "q": 0, "tp": 0, "vf": 0}
            ps1_guard = {}
            prev_h4_te = 0
            g_end_te = 0
            g_end_ve = 0
            g_end_se = 0
            for g0 in range(NG):
                heads = [g0 * GH + i for i in range(GH)]
                VL = load_chunk_major(vgc, v_sp, g0, [("TE", g_end_te)])

                zg = {}
                zev = 0
                W = GH * 128
                ZPB = max(1, 512 // W)
                for lt in range(LT):
                    bank = (lt // ZPB) % 2
                    w0 = [("VE", YT_VE), ("DG", CONST_DG), ("TE", g_end_te),
                          ("SE", g_end_se), ("VE", g_end_ve)]
                    g = zg.get(bank)
                    if lt % ZPB == 0 and g:
                        w0.append(g)
                    t.emit("tensor", lambda e_, lt=lt, bank=bank, g0=g0: e_.matmul(
                        ps[2][:, bank * 512 + (lt % ZPB) * W:
                              bank * 512 + (lt % ZPB) * W + W],
                        yTb[0:LOWR + 1, lt * 128:(lt + 1) * 128],
                        wg2a[0:LOWR + 1, g0 * GH * 128:(g0 + 1) * GH * 128],
                        start=True, stop=True),
                        waits=w0 if lt % ZPB == 0 else [],
                        incs=[("TE", 1)] if (lt % ZPB == ZPB - 1 or lt == LT - 1) else [])
                    if lt % ZPB == ZPB - 1 or lt == LT - 1:
                        tev = t.val("TE")
                        lt0 = (lt // ZPB) * ZPB
                        nw = (lt - lt0 + 1) * W
                        zev = t.emit("vector", lambda e_, lt0=lt0, bank=bank, nw=nw: e_.tensor_copy(
                            Gg[:, lt0 * W:lt0 * W + nw],
                            ps[2][:, bank * 512: bank * 512 + nw]),
                            waits=[("TE", tev), ("VE", g_end_ve)], incs=[("VE", 1)])
                        zg[bank] = ("VE", zev)
                ZW = LT * GH * 128
                sa = t.emit("scalar", lambda e_: e_.activation(
                    Gg[:, 0:ZW], Gg[:, 0:ZW], AF.Sigmoid),
                    waits=[("VE", zev), ("SE", g_end_se)], incs=[("SE", 1)])
                t.emit("scalar", lambda e_: e_.activation(
                    sps[:, 0:ZW], Gg[:, 0:ZW], AF.Ln),
                    waits=[("SE", sa)], incs=[("SE", 1)])
                SP_SE = t.val("SE")

                gslot = {}
                GQ = min(4, NCH)
                for qi in range((GH * NCH) // GQ):
                    bank = qi % 2
                    w0 = [("SE", SP_SE), ("DS", CONST_DS), ("VE", g_end_ve)]
                    g = gslot.get(bank)
                    if g:
                        w0.append(g)
                    for jj in range(GQ):
                        idx = qi * GQ + jj
                        i, j = idx // NCH, idx % NCH
                        t.emit("tensor", lambda e_, i=i, j=j, bank=bank, jj=jj: e_.matmul(
                            ps[0][0:128, bank * 512 + jj * C: bank * 512 + jj * C + C],
                            sps[:, j * GH * 128 + i * 128:j * GH * 128 + (i + 1) * 128],
                            u64[:, :],
                            start=True, stop=True),
                            waits=w0 if jj == 0 else [],
                            incs=[("TE", 1)] if jj == GQ - 1 else [])
                    tev = t.val("TE")
                    i0, j0 = (qi * GQ) // NCH, (qi * GQ) % NCH
                    ev = t.emit("vector", lambda e_, i0=i0, j0=j0, bank=bank: e_.tensor_copy(
                        Gg[:, i0 * T + j0 * C: i0 * T + (j0 + GQ) * C],
                        ps[0][0:128, bank * 512: bank * 512 + GQ * C]),
                        waits=[("TE", tev)], incs=[("VE", 1)])
                    gslot[bank] = ("VE", ev)
                G_SE = t.val("SE")
                G_VE = t.val("VE")

                sdec = t.emit("scalar", lambda e_, h0=heads[0], h1=heads[-1] + 1: e_.activation(
                    dec[:, h0 * NCH:h1 * NCH],
                    Gg[:, C - 1: GH * T: C][:, 0:GH * NCH], AF.Exp),
                    waits=[("VE", G_VE)], incs=[("SE", 1)])
                GW = GH * 32
                m1 = t.emit("vector", lambda e_: e_.memset(glcA[:], 0.0),
                            waits=[("SE", g_end_se), ("VE", g_end_ve)], incs=[("VE", 1)])
                m2 = t.emit("vector", lambda e_: e_.memset(glcB[:], 0.0),
                            waits=[("SE", g_end_se)], incs=[("VE", 1)])
                for i in range(GH):
                    t.emit("vector", lambda e_, i=i: e_.tensor_copy(
                        glcA[:, i * 32:i * 32 + NCH],
                        Gg[:, i * T + C - 1: i * T + T: C][:, 0:NCH]),
                        waits=[("VE", G_VE), ("VE", m1)], incs=[("VE", 1)])
                cur, oth = glcA, glcB
                for sstep in (1, 2, 4, 8):
                    t.emit("vector", lambda e_, cur=cur, oth=oth, sstep=sstep: e_.tensor_tensor(
                        oth[:, 0:GW - sstep], cur[:, 0:GW - sstep],
                        cur[:, sstep:GW], OP.add),
                        waits=[("VE", t.val("VE"))], incs=[("VE", 1)])
                    cur, oth = oth, cur
                SC_VE = t.val("VE")
                sdecS_se = t.emit("scalar", lambda e_, cur=cur: e_.activation(
                    sdecS[:], cur[:], AF.Exp),
                    waits=[("VE", SC_VE)], incs=[("SE", 1)])

                se1 = t.emit("scalar", lambda e_: e_.activation(
                    e1b[:, 0:GH * T], Gg[:, 0:GH * T], AF.Exp, bias=cst[:, 2:3]),
                    waits=[("VE", G_VE), ("DS", prev_sp["e1"]), ("VE", CST_VE)],
                    incs=[("SE", 1)])
                for i, h in enumerate(heads):
                    prev_sp["e1"] = dma("sync", e1_sp.ap()[h],
                                        e1b[:, i * T:(i + 1) * T],
                                        waits=[("SE", se1)])
                se6 = t.emit("scalar", lambda e_: e_.activation(
                    e6b[:, 0:GH * T], Gg[:, 0:GH * T], AF.Exp, scale=-1.0),
                    waits=[("SE", se1)], incs=[("SE", 1)])

                for i, h in enumerate(heads):
                    t.emit("vector", lambda e_, i=i, h=h: e_.tensor_copy(
                        tdst[:, h * (D + 1) + D: h * (D + 1) + D + 1],
                        sdecS[:, i * 32:i * 32 + 1]),
                        waits=[("SE", sdecS_se)], incs=[("VE", 1)])

                    kl = dma("sync", hk[:], k_sp.ap()[h],
                             waits=[("VE", prev_sp["vf"])], ch=1)
                    vf = t.emit("vector", lambda e_, i=i: e_.tensor_tensor(
                        ktf[:], hk[:], e6b[:, i * T:(i + 1) * T], OP.mult),
                        waits=[("DS2", kl), ("SE", se6), ("TE", g_end_te)],
                        incs=[("VE", 1)])
                    prev_sp["vf"] = vf
                    vb2 = t.emit("vector", lambda e_: e_.tensor_copy(ktb[:], ktf[:]),
                                 waits=[("VE", vf), ("DS", prev_sp["kt"])],
                                 incs=[("VE", 1)])
                    prev_sp["kt"] = dma("sync", kt_sp.ap()[h], ktb[:],
                                        waits=[("VE", vb2)])

                    for j in range(NCH):
                        t.emit("vector", lambda e_, j=j, i=i: e_.tensor_scalar(
                            ktb2[:, j * C:(j + 1) * C], ktf[:, j * C:(j + 1) * C],
                            sdecS[:, i * 32 + j: i * 32 + j + 1], None, OP.mult),
                            waits=[("VE", vf), ("SE", sdecS_se),
                                   ("TE", prev_h4_te)] if j == 0 else [],
                            incs=[("VE", 1)] if j == NCH - 1 else [])
                    KS_VE = t.val("VE")
                    NQ = (NCH + 3) // 4
                    tr_g = {}
                    for q in range(NQ):
                        bank = q % 2
                        nq = min(4, NCH - q * 4)
                        trw = [("VE", vf), ("VE", G_VE)]
                        g = tr_g.get(bank)
                        if g:
                            trw.append(g)
                        for jj in range(nq):
                            j = q * 4 + jj
                            t.emit("tensor", lambda e_, j=j, bank=bank, jj=jj: e_.transpose(
                                ps[3][:, bank * 512 + jj * 128: bank * 512 + jj * 128 + 128],
                                ktf[:, j * C:(j + 1) * C], ident[:]),
                                waits=trw if jj == 0 else [],
                                incs=[("TE", 1)] if jj == nq - 1 else [])
                        tev = t.val("TE")
                        ev = t.emit("scalar", lambda e_, q=q, bank=bank, nq=nq: e_.copy(
                            ktm[:, q * 4 * D: q * 4 * D + nq * D],
                            ps[3][:, bank * 512: bank * 512 + nq * 128]),
                            waits=[("TE", tev), ("TE", prev_h4_te)], incs=[("SE", 1)])
                        tr_g[bank] = ("SE", ev)
                    TR_SE = t.val("SE")
                    trp_ev = {}
                    trp_g = {}
                    for q in range(NQ):
                        bank = q % 2
                        nq = min(4, NCH - q * 4)
                        trw = [("VE", KS_VE), ("VE", G_VE)]
                        g = trp_g.get(bank)
                        if g:
                            trw.append(g)
                        for jj in range(nq):
                            j = q * 4 + jj
                            t.emit("tensor", lambda e_, j=j, bank=bank, jj=jj: e_.transpose(
                                ps[0][:, bank * 512 + jj * 128: bank * 512 + jj * 128 + 128],
                                ktb2[:, j * C:(j + 1) * C], ident[:]),
                                waits=trw if jj == 0 else [],
                                incs=[("TE", 1)] if jj == nq - 1 else [])
                        tev = t.val("TE")
                        ev = t.emit("vector", lambda e_, q=q, bank=bank, nq=nq: e_.tensor_copy(
                            ktm2[:, q * 4 * D: q * 4 * D + nq * D],
                            ps[0][:, bank * 512: bank * 512 + nq * 128]),
                            waits=[("TE", tev), ("TE", prev_h4_te)], incs=[("VE", 1)])
                        trp_ev[q] = ("VE", t.val("VE"))
                        trp_g[bank] = ("VE", t.val("VE"))
                    FSL = [(2, 0), (2, 1)]
                    tp_guard = {}
                    for q in range(NQ):
                        pi, hf = FSL[q % 2]
                        nq = min(4, NCH - q * 4)
                        w0 = [("SE", TR_SE), ("DS", VL)]
                        g = tp_guard.get(q % 2)
                        if g:
                            w0.append(g)
                        for jj in range(nq):
                            j = q * 4 + jj
                            t.emit("tensor", lambda e_, j=j, pi=pi, hf=hf, jj=jj, i=i: e_.matmul(
                                ps[pi][0:D, hf * 512 + jj * D: hf * 512 + (jj + 1) * D],
                                ktm[:, j * D:(j + 1) * D],
                                vgc[:, j * GHD + i * D: j * GHD + (i + 1) * D],
                                start=True, stop=True),
                                waits=w0 if jj == 0 else [],
                                incs=[("TE", 1)] if jj == nq - 1 else [])
                        tev = t.val("TE")
                        sq = q % 2
                        ev2 = t.emit("scalar", lambda e_, sq=sq, pi=pi, hf=hf, nq=nq: e_.copy(
                            tps[:, sq * 512: sq * 512 + nq * D],
                            ps[pi][0:D, hf * 512: hf * 512 + nq * D]),
                            waits=[("TE", tev), ("DS", prev_sp["tp"])], incs=[("SE", 1)])
                        tp_guard[q % 2] = ("SE", ev2)
                        if q % 2 == 1 or q == NQ - 1:
                            j = q * 4 + nq - 1
                            j0 = (j // 8) * 8
                            nj = j - j0 + 1
                            prev_sp["tp"] = dma(
                                "sync",
                                tp_sp.ap()[h][j0:j + 1].rearrange("j a b -> a j b"),
                                tps[:, 0:nj * D].rearrange("a (j b) -> a j b", j=nj),
                                waits=[("SE", t.val("SE"))])
                    abank = i % 2
                    for j in range(NCH):
                        w0 = []
                        if j % 4 == 0:
                            w0 = [trp_ev[j // 4]]
                        if j == 0:
                            w0 = w0 + [("DS", VL)]
                            g = ps1_guard.get(abank)
                            if g:
                                w0.append(g)
                        t.emit("tensor", lambda e_, j=j, abank=abank, i=i: e_.matmul(
                            ps[1][0:D, abank * 512: abank * 512 + D],
                            ktm2[:, j * D:(j + 1) * D],
                            vgc[:, j * GHD + i * D: j * GHD + (i + 1) * D],
                            start=(j == 0), stop=(j == NCH - 1)),
                            waits=w0,
                            incs=[("TE", 1)] if j == NCH - 1 else [])
                    acc_te = t.val("TE")
                    sfv = t.emit("vector", lambda e_, h=h, abank=abank: e_.tensor_copy(
                        tdst[:, h * (D + 1): h * (D + 1) + D],
                        ps[1][0:D, abank * 512: abank * 512 + D]),
                        waits=[("TE", acc_te)], incs=[("VE", 1)])
                    ps1_guard[abank] = ("VE", sfv)
                    prev_h4_te = t.val("TE")
                g_end_te = t.val("TE")
                g_end_ve = t.val("VE")
                g_end_se = t.val("SE")
            TD_VE, TD_SE = t.val("VE"), t.val("SE")
            tap("tdst", tdst[:], (128, H * (D + 1)), F32,
                [("VE", TD_VE), ("SE", TD_SE)])
            tdw = dma("sync", td_in.ap(), tdst[:], waits=[("VE", TD_VE), ("SE", TD_SE)])
            t.emit("gpsimd", lambda e_: e_.collective_compute(
                "AllGather", OP.bypass, replica_groups=[list(range(8))],
                ins=[td_in.ap()], outs=[td_out.ap()]),
                waits=[("DS", tdw)], incs=[("CC1", 1)])
            t.barrier()
            tapd("q_sp", q_sp, [("DS", t.val("DS"))])
            tapd("kt_sp", kt_sp, [("DS", t.val("DS"))])
            tapd("e1_sp", e1_sp, [("DS", t.val("DS"))])
            tapd("tp_sp", tp_sp, [("DS", t.val("DS"))])
        ln1T_scope.close()

        with ExitStack() as phB:
            eB = phB.enter_context
            tdr = eB(nc.sbuf_tensor("tdr", [128, 8 * H * (D + 1)], F32))
            dmc = eB(nc.sbuf_tensor("dmc", [128, 8 * H], F32))
            tmsk = eB(nc.sbuf_tensor("tmsk", [128, D], F32))

            RW = H * (D + 1)
            tl = dma("sync", tdr[:].rearrange("p (r c) -> p r c", r=8),
                     td_out.ap().rearrange("(r p) c -> p r c", p=128),
                     waits=[("CC1", 1)])
            t.wait("vector", [("DS", tl)])
            for r in range(8):
                for h in range(H):
                    v1 = t.emit("vector", lambda e_, r=r, h=h: e_.tensor_tensor(
                        dmc[:, r * H + h: r * H + h + 1],
                        tdr[:, r * RW + h * (D + 1) + D: r * RW + h * (D + 1) + D + 1],
                        fm[:, r:r + 1], OP.mult), incs=[("VE", 1)])
                    t.emit("vector", lambda e_, r=r, h=h: e_.tensor_tensor(
                        dmc[:, r * H + h: r * H + h + 1],
                        dmc[:, r * H + h: r * H + h + 1],
                        fmi[:, r:r + 1], OP.add),
                        waits=[("VE", v1)], incs=[("VE", 1)])
            DMC_VE = t.val("VE")
            t.wait("vector", [("VE", DMC_VE)])
            for h in range(H):
                cv = t.emit("vector", lambda e_, h=h: e_.memset(
                    sin_f[:, h * D:(h + 1) * D], 0.0), incs=[("VE", 1)])
                for r in range(8):
                    c1 = t.emit("vector", lambda e_, r=r, h=h: e_.tensor_scalar(
                        sin_f[:, h * D:(h + 1) * D], sin_f[:, h * D:(h + 1) * D],
                        dmc[:, r * H + h: r * H + h + 1], None, OP.mult),
                        waits=[("VE", cv)], incs=[("VE", 1)])
                    c2 = t.emit("vector", lambda e_, r=r, h=h: e_.tensor_scalar(
                        tmsk[:], tdr[:, r * RW + h * (D + 1): r * RW + h * (D + 1) + D],
                        fm[:, r:r + 1], None, OP.mult),
                        waits=[("VE", c1)], incs=[("VE", 1)])
                    cv = t.emit("vector", lambda e_, h=h: e_.tensor_tensor(
                        sin_f[:, h * D:(h + 1) * D], sin_f[:, h * D:(h + 1) * D],
                        tmsk[:], OP.add),
                        waits=[("VE", c2)], incs=[("VE", 1)])
            SIN_VE = t.val("VE")
            tap("sin", sin_f[:], (128, H * D), F32, [("VE", SIN_VE)])
            t.barrier()

        with ExitStack() as phC:
            eC = phC.enter_context
            hq2 = eC(nc.sbuf_tensor("hq2", [128, 2 * T], BF16))
            he1 = eC(nc.sbuf_tensor("he1", [128, 2 * T], BF16))
            hkt = eC(nc.sbuf_tensor("hkt", [128, 2 * T], BF16))
            hqg = eC(nc.sbuf_tensor("hqg", [128, 2 * T], BF16))
            vgc2 = eC(nc.sbuf_tensor("vgc2", [128, NCH * GHD], BF16))
            ggc = eC(nc.sbuf_tensor("ggc", [128, NCH * GHD], BF16))
            tpr = eC(nc.sbuf_tensor("tpr", [128, 2 * 8 * D], F32))
            sbj = eC(nc.sbuf_tensor("sbj", [128, NCH * D], BF16))
            amb = eC(nc.sbuf_tensor("amb", [128, NCH * C], BF16))
            ost = eC(nc.sbuf_tensor("ost", [128, NCH * GHD], F32))
            rst = eC(nc.sbuf_tensor("rst", [128, 2 * NCH * GH], F32))
            sgb = eC(nc.sbuf_tensor("sgb", [128, NCH * GHD], BF16))
            mst = eC(nc.sbuf_tensor("mst", [128, NCH * GHD], F32))

            prev_norm = ("VE", 0)
            prev_norm_se = ("SE", 0)
            prev_mspill = 0
            g_end_te2 = 0
            for g0 in range(NG):
                heads = [g0 * GH + i for i in range(GH)]
                VL = load_chunk_major(vgc2, v_sp, g0, [("TE", g_end_te2)])
                GL = load_chunk_major(ggc, g_sp, g0, [("SE", prev_norm_se[1]),
                                                      ("VE", prev_norm[1])])
                o_guard = {}
                a_guard = {}
                amb_guard = {}
                h_end_te = g_end_te2
                hprev = {0: g_end_te2, 1: g_end_te2}
                prev_ch_ve = t.val("VE")
                for i, h in enumerate(heads):
                    hp = i % 2
                    off = hp * T
                    ql = dma("scalar", hq2[:, off:off + T], q_sp.ap()[h],
                             waits=[("TE", hprev[hp])], ch=0)
                    ql2 = dma("scalar", he1[:, off:off + T], e1_sp.ap()[h],
                              waits=[("TE", hprev[hp])], ch=1)
                    ql3 = dma("sync", hkt[:, off:off + T], kt_sp.ap()[h],
                              waits=[("TE", hprev[hp])], ch=1)
                    vqg = t.emit("vector", lambda e_, off=off: e_.tensor_tensor(
                        hqg[:, off:off + T], hq2[:, off:off + T], he1[:, off:off + T],
                        OP.mult),
                        waits=[("DA", ql), ("DA2", ql2), ("DS2", ql3),
                               ("VE", SIN_VE), ("TE", hprev[hp])],
                        incs=[("VE", 1)])
                    vcp = t.emit("vector", lambda e_, h=h: e_.tensor_copy(
                        s_f[:], sin_f[:, h * D:(h + 1) * D]),
                        waits=[("VE", SIN_VE)], incs=[("VE", 1)])
                    ch_v = t.emit("vector", lambda e_: e_.tensor_copy(
                        sbj[:, 0:D], s_f[:]),
                        waits=[("VE", vcp), ("TE", g_end_te2)], incs=[("VE", 1)])
                    NB8 = (NCH + 7) // 8
                    batch_chain = {}
                    batch_chain[-2] = batch_chain[-1] = prev_ch_ve
                    tl8s = {}

                    def tpr_load(j8):
                        nj8 = min(8, NCH - j8 * 8)
                        bp = j8 % 2
                        tl8s[j8] = dma("sync",
                                       tpr[:, bp * 8 * D: bp * 8 * D + nj8 * D]
                                       .rearrange("a (j b) -> a j b", j=nj8),
                                       tp_sp.ap()[h][j8 * 8:j8 * 8 + nj8]
                                       .rearrange("j a b -> a j b"),
                                       waits=[("VE", batch_chain[j8 - 2])],
                                       ch=0)

                    tpr_load(0)
                    if NB8 > 1:
                        tpr_load(1)
                    for j8 in range(NB8):
                        nj8 = min(8, NCH - j8 * 8)
                        bp = j8 % 2
                        for jj in range(nj8):
                            j = j8 * 8 + jj
                            c1 = t.emit("vector", lambda e_, jj=jj, bp=bp: e_.tensor_tensor(
                                stmp[:], s_f[:],
                                tpr[:, bp * 8 * D + jj * D: bp * 8 * D + (jj + 1) * D],
                                OP.add),
                                waits=[("DS", tl8s[j8]), ("VE", ch_v)], incs=[("VE", 1)])
                            c2 = t.emit("vector", lambda e_, h=h, j=j: e_.tensor_scalar(
                                s_f[:], stmp[:], dec[:, h * NCH + j: h * NCH + j + 1],
                                None, OP.mult),
                                waits=[("VE", c1)], incs=[("VE", 1)])
                            ch_v = c2
                            if j < NCH - 1:
                                ch_v = t.emit("vector", lambda e_, j=j: e_.tensor_copy(
                                    sbj[:, (j + 1) * D:(j + 2) * D], s_f[:]),
                                    waits=[("VE", c2)], incs=[("VE", 1)])
                        batch_chain[j8] = ch_v
                        if j8 + 2 < NB8:
                            tpr_load(j8 + 2)
                    CH_VE = ch_v
                    prev_ch_ve = ch_v
                    ASL = [(0, 0), (0, 1), (2, 0), (2, 1)]
                    mv_j = {}
                    for j in range(NCH):
                        pi, hf = ASL[j % 4]
                        w0 = [("VE", vqg)]
                        g = a_guard.get(j % 4)
                        if g:
                            w0.append(g)
                        t.emit("tensor", lambda e_, j=j, pi=pi, hf=hf, off=off: e_.matmul(
                            ps[pi][:, hf * 512: hf * 512 + C],
                            hkt[:, off + j * C: off + (j + 1) * C],
                            hqg[:, off + j * C: off + (j + 1) * C],
                            start=True, stop=True),
                            waits=w0, incs=[("TE", 1)])
                        tev = t.val("TE")
                        mv = t.emit("vector", lambda e_, j=j, pi=pi, hf=hf: e_.tensor_tensor(
                            amb[:, j * C:(j + 1) * C],
                            ps[pi][:, hf * 512: hf * 512 + C],
                            trim[:], OP.mult),
                            waits=[("TE", tev), ("DS", CONST_DS)], incs=[("VE", 1)])
                        a_guard[j % 4] = ("VE", mv)
                        mv_j[j] = mv
                    OSL = [(1, 0), (1, 1), (3, 0), (3, 1)]
                    for j in range(NCH):
                        pi, hf = OSL[j % 4]
                        w1 = [("VE", mv_j[j]), ("DS", VL), ("VE", CH_VE)]
                        g = o_guard.get(j % 4)
                        if g:
                            w1.append(g)
                        t.emit("tensor", lambda e_, j=j, i=i, pi=pi, hf=hf: e_.matmul(
                            ps[pi][:, hf * 512: hf * 512 + D],
                            amb[:, j * C:(j + 1) * C],
                            vgc2[:, j * GHD + i * D: j * GHD + (i + 1) * D],
                            start=True, stop=False),
                            waits=w1, incs=[("TE", 1)])
                        t.emit("tensor", lambda e_, j=j, pi=pi, hf=hf, off=off: e_.matmul(
                            ps[pi][:, hf * 512: hf * 512 + D],
                            hqg[:, off + j * C: off + (j + 1) * C],
                            sbj[:, j * D:(j + 1) * D],
                            start=False, stop=True),
                            waits=[], incs=[("TE", 1)])
                        tev = t.val("TE")
                        oev = t.emit("scalar", lambda e_, j=j, i=i, pi=pi, hf=hf: e_.copy(
                            ost[:, j * GHD + i * D: j * GHD + (i + 1) * D],
                            ps[pi][:, hf * 512: hf * 512 + D]),
                            waits=[("TE", tev), prev_norm], incs=[("SE", 1)])
                        o_guard[j % 4] = ("SE", oev)
                    h_end_te = t.val("TE")
                    hprev[hp] = h_end_te
                O_SE = t.val("SE")
                sq_se = 0
                for j in range(NCH):
                    for i2 in range(GH):
                        sq_se = t.emit("scalar", lambda e_, j=j, i2=i2: e_.activation(
                            mst[:, j * GHD + i2 * D: j * GHD + (i2 + 1) * D],
                            ost[:, j * GHD + i2 * D: j * GHD + (i2 + 1) * D],
                            AF.Square, accum_out=rst[:, j * GH + i2: j * GH + i2 + 1]),
                            waits=[("SE", O_SE), ("VE", prev_norm[1]),
                                   ("DS", prev_mspill)], incs=[("SE", 1)])
                s_ln = t.emit("scalar", lambda e_: e_.activation(
                    rst[:, NCH * GH:2 * NCH * GH], rst[:, 0:NCH * GH], AF.Ln,
                    scale=1.0 / D, bias=cst[:, 3:4]),
                    waits=[("SE", sq_se), ("VE", CST_VE)], incs=[("SE", 1)])
                s_ex = t.emit("scalar", lambda e_: e_.activation(
                    rst[:, 0:NCH * GH], rst[:, NCH * GH:2 * NCH * GH], AF.Exp,
                    scale=-0.5),
                    waits=[("SE", s_ln)], incs=[("SE", 1)])
                s_si = t.emit("scalar", lambda e_: e_.activation(
                    sgb[:], ggc[:], AF.Silu),
                    waits=[("DS", GL), ("SE", s_ex), ("VE", prev_norm[1])],
                    incs=[("SE", 1)])
                ap_v = 0
                for j in range(NCH):
                    for i2 in range(GH):
                        ap_v = t.emit("vector", lambda e_, j=j, i2=i2: e_.tensor_scalar(
                            mst[:, j * GHD + i2 * D: j * GHD + (i2 + 1) * D],
                            ost[:, j * GHD + i2 * D: j * GHD + (i2 + 1) * D],
                            rst[:, j * GH + i2: j * GH + i2 + 1], None, OP.mult),
                            waits=[("SE", s_ex), ("DS", prev_mspill)], incs=[("VE", 1)])
                sil_v = t.emit("vector", lambda e_: e_.tensor_tensor(
                    mst[:], mst[:], sgb[:], OP.mult),
                    waits=[("SE", s_si), ("VE", ap_v)], incs=[("VE", 1)])
                NRM_VE = t.val("VE")
                dst3 = m_sp.ap().rearrange("lt p c -> p lt c")
                for hlf in range(2):
                    prev_mspill = dma("sync",
                                      dst3[:, hlf * (NCH // 2):(hlf + 1) * (NCH // 2),
                                           g0 * GHD:(g0 + 1) * GHD],
                                      mst[:].rearrange("p (j c) -> p j c", j=NCH)
                                      [:, hlf * (NCH // 2):(hlf + 1) * (NCH // 2), :],
                                      waits=[("VE", NRM_VE)])
                prev_norm = ("VE", NRM_VE)
                prev_norm_se = ("SE", t.val("SE"))
                g_end_te2 = t.val("TE")
            t.barrier()
            tapd("m_sp", m_sp, [("DS", t.val("DS"))])


    ln2T = None
    with ExitStack() as phDE:
        eDE = phDE.enter_context
        ln2T = eDE(nc.sbuf_tensor("ln2T", [128, H * T], BF16))

        with ExitStack() as phD:
            eD = phD.enter_context
            wof = eD(nc.sbuf_tensor("wof", [128, H * DM], BF16))
            mlt = eD(nc.sbuf_tensor("mlt", [128, DM], F32))
            mTs = eD(nc.sbuf_tensor("mTs", [128, DM], BF16))
            dsb = eD(nc.sbuf_tensor("dsb", [128, 2 * DM], F32))
            x2s = eD(nc.sbuf_tensor("x2s", [128, DM], F32))
            ln2s = eD(nc.sbuf_tensor("ln2s", [128, DM], F32))

            for kt in range(H):
                wl = dma("gpsimd", wof[:, kt * DM:(kt + 1) * DM],
                         Wo.rearrange("(kt p) c -> kt p c", p=128)[kt])
                t.emit("vector", lambda e_, kt=kt: e_.tensor_scalar(
                    wof[:, kt * DM:(kt + 1) * DM], wof[:, kt * DM:(kt + 1) * DM],
                    nwcol[:, 0:1], None, OP.mult),
                    waits=[("DG", wl), ("DS", CONST_DS)], incs=[("VE", 1)])
            WOF_VE = t.val("VE")

            prev_lt = {"mTs": ("SE", 0), "ps": None, "mlt": ("SE", 0)}
            for lt in range(LT):
                ml = dma("sync", mlt[:], m_sp.ap()[lt], waits=[prev_lt["mTs"]])
                t.wait("tensor", [("DS", ml)])
                for kt in range(H):
                    emit_tr(mlt[:, kt * 128:(kt + 1) * 128],
                            mTs[:, kt * 128:(kt + 1) * 128],
                            [("DS", ml), prev_lt["mTs"]])
                MT_SE = t.val("SE")
                prev_lt["mTs"] = ("SE", MT_SE)
                w0 = [("SE", MT_SE), ("VE", WOF_VE)]
                if prev_lt["ps"]:
                    w0.append(prev_lt["ps"])
                for kt in range(H):
                    for sl in range(DM // 512):
                        t.emit("tensor", lambda e_, kt=kt, sl=sl: e_.matmul(
                            ps[sl // 2][:, (sl % 2) * 512:(sl % 2) * 512 + 512],
                            mTs[:, kt * 128:(kt + 1) * 128],
                            wof[:, kt * DM + sl * 512: kt * DM + (sl + 1) * 512],
                            start=(kt == 0), stop=(kt == H - 1)),
                            waits=w0 if (kt == 0 and sl == 0) else [],
                            incs=[("TE", 1)] if (kt == H - 1 and sl == DM // 512 - 1) else [])
                tev = t.val("TE")
                ev1 = t.emit("vector", lambda e_: e_.tensor_copy(
                    dsb[:, 0:min(1024, DM)], ps[0][:, 0:min(1024, DM)]),
                    waits=[("TE", tev)], incs=[("VE", 1)])
                if DM > 1024:
                    ev1 = t.emit("vector", lambda e_: e_.tensor_copy(
                        dsb[:, 1024:DM], ps[1][:, 0:DM - 1024]),
                        waits=[("TE", tev)], incs=[("VE", 1)])
                prev_lt["ps"] = ("VE", ev1)
                t.wait("vector", [("VE", ev1)])
                t.wait("scalar", [("VE", ev1)])
                l3 = emit_ln(dsb[:, 0:DM], dsb[:, DM:2 * DM], DM, cfg.LNEPS)
                xl = dma("sync", x2s[:], x_t[lt], waits=[("VE", l3)])
                vx = t.emit("vector", lambda e_: e_.tensor_tensor(
                    x2s[:], x2s[:], dsb[:, DM:2 * DM], OP.add),
                    waits=[("DS", xl), ("VE", l3)], incs=[("VE", 1)])
                dma("sync", x2_sp.ap()[lt], x2s[:], waits=[("VE", vx)])
                t.wait("vector", [("VE", vx)])
                t.wait("scalar", [("VE", vx)])
                l2 = emit_ln(x2s[:], ln2s[:], DM, cfg.LNEPS)
                for kt in range(H):
                    emit_tr(ln2s[:, kt * 128:(kt + 1) * 128],
                            ln2T[:, kt * T + lt * 128: kt * T + (lt + 1) * 128],
                            [("VE", l2)])
            LN2T_SE = t.val("SE")
            t.barrier()
            tapd("x2_sp", x2_sp, [("DS", t.val("DS"))])
            tapd("wg_all", wg_all, [("CC2", 1)])
            tapd("wd_all", wd_all, [("CC4", 1)])
            tap("x2", x2s[:], (128, DM), F32, [("VE", t.val("VE"))])
            tap("ln2T", ln2T[:], (128, H * T), BF16, [("SE", LN2T_SE)])

        with ExitStack() as phE:
            eE = phE.enter_context
            hact = eE(nc.sbuf_tensor("hact", [128, OIT_REAL * NHF], BF16))
            wst = eE(nc.sbuf_tensor("wst", [128, 4 * DM], BF16))
            wdsl = eE(nc.sbuf_tensor("wdsl", [128, 2 * DM], BF16))
            sgE = eE(nc.sbuf_tensor("sgE", [128, NHF], BF16))
            odsb = eE(nc.sbuf_tensor("odsb", [128, 4 * DM], F32))
            x2r = eE(nc.sbuf_tensor("x2r", [128, DM], F32))

            wg_t = wg_all.ap()
            wu_t = wu_all.ap()
            wd_t = wd_all.ap()
            NSL = (NHF + 511) // 512

            wslot_guard = {}
            for half in range(NSEC):
                snap = ("VE", t.val("VE"))
                psg = {0: snap, 1: snap, 2: snap, 3: snap}
                prev_vha = 0
                for ot in range(OIT_REAL):
                    pg = [0, 2][ot % 2]
                    pu = [1, 3][ot % 2]
                    sl_w = ot % 2
                    wlg = dma("sync", wst[:, sl_w * 2 * DM: sl_w * 2 * DM + DM],
                              wg_t[ot * 128:(ot + 1) * 128, :],
                              waits=[("CC2", 1), ("TE", wslot_guard.get(sl_w, 0))],
                              ch=sl_w)
                    wlu = dma("scalar", wst[:, sl_w * 2 * DM + DM: sl_w * 2 * DM + 2 * DM],
                              wu_t[ot * 128:(ot + 1) * 128, :],
                              waits=[("CC3", 1), ("TE", wslot_guard.get(sl_w, 0))],
                              ch=sl_w)
                    w0 = [("DS2" if sl_w else "DS", wlg),
                          ("DA2" if sl_w else "DA", wlu), ("SE", LN2T_SE)]
                    if psg.get(pg):
                        w0.append(psg[pg])
                    if psg.get(pu):
                        w0.append(psg[pu])
                    for kt in range(H):
                        for sl in range(NSL):
                            wdt = min(512, NHF - sl * 512)
                            t.emit("tensor", lambda e_, kt=kt, sl=sl, wdt=wdt, pg=pg, half=half, sl_w=sl_w: e_.matmul(
                                ps[pg][:, sl * 512: sl * 512 + wdt],
                                wst[:, sl_w * 2 * DM + kt * 128: sl_w * 2 * DM + (kt + 1) * 128],
                                ln2T[:, kt * T + half * NHF + sl * 512:
                                     kt * T + half * NHF + sl * 512 + wdt],
                                start=(kt == 0), stop=(kt == H - 1)),
                                waits=w0 if (kt == 0 and sl == 0) else [],
                                incs=[("TE", 1)] if (kt == H - 1 and sl == NSL - 1) else [])
                    for kt in range(H):
                        for sl in range(NSL):
                            wdt = min(512, NHF - sl * 512)
                            t.emit("tensor", lambda e_, kt=kt, sl=sl, wdt=wdt, pu=pu, half=half, sl_w=sl_w: e_.matmul(
                                ps[pu][:, sl * 512: sl * 512 + wdt],
                                wst[:, sl_w * 2 * DM + DM + kt * 128: sl_w * 2 * DM + DM + (kt + 1) * 128],
                                ln2T[:, kt * T + half * NHF + sl * 512:
                                     kt * T + half * NHF + sl * 512 + wdt],
                                start=(kt == 0), stop=(kt == H - 1)),
                                waits=[],
                                incs=[("TE", 1)] if (kt == H - 1 and sl == NSL - 1) else [])
                    tev = t.val("TE")
                    wslot_guard[sl_w] = tev
                    ssi = t.emit("scalar", lambda e_, pg=pg: e_.activation(
                        sgE[:, 0:NHF], ps[pg][:, 0:NHF], AF.Silu),
                        waits=[("TE", tev), ("VE", prev_vha)], incs=[("SE", 1)])
                    vha = t.emit("vector", lambda e_, ot=ot, pu=pu: e_.tensor_tensor(
                        hact[:, ot * NHF:(ot + 1) * NHF], sgE[:, 0:NHF],
                        ps[pu][:, 0:NHF], OP.mult),
                        waits=[("SE", ssi)], incs=[("VE", 1)])
                    psg[pg] = ("SE", ssi)
                    psg[pu] = ("VE", vha)
                    prev_vha = vha
                HA_VE = t.val("VE")
                if half == 0:
                    tap("hact", hact[:], (128, OIT_REAL * NHF), BF16,
                        [("VE", HA_VE)])

                NQD = (LT // NSEC + 3) // 4
                NDMH = 2 if DM > 1024 else 1
                DMH = DM // NDMH
                ps_guard = None
                for quad in range(NQD):
                    lts = [half * (LT // NSEC) + quad * 4 + i for i in range(4)
                           if quad * 4 + i < LT // NSEC]
                    for dmh in range(NDMH):
                        wd_guard = {}
                        for ot in range(OIT_REAL):
                            sl_d = ot % 2
                            qn = "sync" if sl_d == 0 else "scalar"
                            sem = ("DS2" if sl_d else "DS") if qn == "sync" else ("DA2" if sl_d else "DA")
                            wl = dma(qn, wdsl[:, sl_d * DM: sl_d * DM + DMH],
                                     wd_t[ot * 128:(ot + 1) * 128,
                                          dmh * DMH:(dmh + 1) * DMH],
                                     waits=[("CC4", 1), ("TE", wd_guard.get(sl_d, 0)),
                                            ("VE", HA_VE), ps_guard], ch=sl_d)
                            w0 = [(sem, wl)]
                            if ps_guard:
                                w0.append(ps_guard)
                            for li, lt in enumerate(lts):
                                ltl = lt - half * (LT // NSEC)
                                for sl in range(DMH // 512):
                                    t.emit("tensor", lambda e_, ot=ot, sl=sl, ltl=ltl, li=li, sl_d=sl_d: e_.matmul(
                                        ps[(li * (DMH // 512) + sl) // 2][:, ((li * (DMH // 512) + sl) % 2) * 512:((li * (DMH // 512) + sl) % 2) * 512 + 512],
                                        hact[:, ot * NHF + ltl * 128: ot * NHF + (ltl + 1) * 128],
                                        wdsl[:, sl_d * DM + sl * 512: sl_d * DM + (sl + 1) * 512],
                                        start=(ot == 0), stop=(ot == OIT_REAL - 1)),
                                        waits=w0 if (sl == 0 and li == 0) else [],
                                        incs=[("TE", 1)] if (sl == DMH // 512 - 1 and li == len(lts) - 1) else [])
                            wd_guard[sl_d] = t.val("TE")
                        tev = t.val("TE")
                        for li, lt in enumerate(lts):
                            base = li * (DMH // 512)
                            ev1 = t.emit("vector", lambda e_, li=li, dmh=dmh, base=base: e_.tensor_copy(
                                odsb[:, li * DM + dmh * DMH: li * DM + dmh * DMH + DMH],
                                ps[base // 2][:, (base % 2) * 512:(base % 2) * 512 + DMH]),
                                waits=[("TE", tev)], incs=[("VE", 1)])
                        ps_guard = ("VE", t.val("VE"))
                    for li, lt in enumerate(lts):
                        t.wait("vector", [ps_guard])
                        t.wait("scalar", [ps_guard])
                        l4 = emit_ln(odsb[:, li * DM:(li + 1) * DM],
                                     odsb[:, li * DM:(li + 1) * DM], DM, cfg.LNEPS)
                        xl = dma("sync", x2r[:], x2_sp.ap()[lt], waits=[("VE", l4)])
                        vo = t.emit("vector", lambda e_, li=li: e_.tensor_tensor(
                            x2r[:], x2r[:], odsb[:, li * DM:(li + 1) * DM], OP.add),
                            waits=[("DS", xl), ("VE", l4)], incs=[("VE", 1)])
                        dma("sync", out_t[lt], x2r[:], waits=[("VE", vo)])
                    ps_guard = ("VE", t.val("VE"))
            t.wait("sync", [("DS", t.val("DS")), ("DA", t.val("DA")),
                            ("DG", t.val("DG"))])

    t.finalize(block)
    ctx.close()
    return nc, dbg



def _consts(cfg, core):
    C = cfg.C
    ident = np.eye(128, dtype=np.float32)
    r = np.arange(C)
    u64 = np.where(r[:, None] <= r[None, :], 1.0 / 16.0, 0.0).astype(ml_dtypes.bfloat16)
    trimm = np.where(r[:, None] <= r[None, :], 1.0, 0.0).astype(ml_dtypes.bfloat16)
    fmv = np.zeros((128, 8), np.float32)
    seq, span = core // 4, core % 4
    for rk in range(8):
        if rk // 4 == seq and rk % 4 < span:
            fmv[:, rk] = 1.0
    return ident, u64, trimm, fmv, 1.0 - fmv


def kernel(L0_attn_x, L0_attn_Wq, L0_attn_Wk, L0_attn_Wv, L0_attn_Wg1,
           L0_attn_Wg2, L0_attn_bg2, L0_attn_Wgate, L0_attn_norm_w, L0_attn_Wo,
           L0_ffn_lut, L0_ffn_gate_idx, L0_ffn_up_idx, L0_ffn_down_idx):
    cfg = Cfg()
    nc, _ = build_core(cfg)

    x = np.ascontiguousarray(np.asarray(L0_attn_x, np.float32).reshape(2 * 8192, 2048))
    lut = np.asarray(L0_ffn_lut, np.float32).reshape(16)
    lutb = np.tile(lut[None, :], (128, 1)).astype(np.float32)
    gi = np.asarray(L0_ffn_gate_idx, np.int32)
    ui = np.asarray(L0_ffn_up_idx, np.int32)
    di = np.asarray(L0_ffn_down_idx, np.int32)
    INTER = cfg.INTER
    PADR = cfg.OIT_PC * 128 * 8

    def pad_rows(a):
        return np.concatenate(
            [a, np.zeros((PADR - a.shape[0], a.shape[1]), np.int32)], axis=0)

    gi_p = pad_rows(gi)
    ui_p = pad_rows(ui)
    diT = np.ascontiguousarray(di.T)
    diT_p = pad_rows(diT)

    shared = {
        "Wq": np.asarray(L0_attn_Wq, np.float32),
        "Wk": np.asarray(L0_attn_Wk, np.float32),
        "Wv": np.asarray(L0_attn_Wv, np.float32),
        "Wg1": np.asarray(L0_attn_Wg1, np.float32),
        "Wg2b": np.concatenate([np.asarray(L0_attn_Wg2, np.float32),
                                np.asarray(L0_attn_bg2, np.float32).reshape(1, 2048)], 0),
        "Wgate": np.asarray(L0_attn_Wgate, np.float32),
        "norm_w": np.asarray(L0_attn_norm_w, np.float32).reshape(128, 1),
        "Wo": np.asarray(L0_attn_Wo, np.float32),
        "lutb": lutb,
    }
    in_maps = []
    NR = cfg.OIT_PC * 128
    for c in range(8):
        ident, u64, trimm, fmv, fmiv = _consts(cfg, c)
        seq, span = c // 4, c % 4
        sl = slice(seq * 8192 + span * 2048, seq * 8192 + (span + 1) * 2048)
        m = dict(shared)
        m.update({
            "x": np.ascontiguousarray(x[sl]),
            "gidx": np.ascontiguousarray(gi_p[c * NR:(c + 1) * NR]),
            "uidx": np.ascontiguousarray(ui_p[c * NR:(c + 1) * NR]),
            "didxT": np.ascontiguousarray(diT_p[c * NR:(c + 1) * NR]),
            "identc": ident, "u64c": u64, "trimc": trimm,
            "fmc": fmv, "fmic": fmiv,
        })
        in_maps.append(m)

    res = run_bass_kernel_spmd(nc, in_maps, core_ids=list(range(8)))
    outp = np.empty((2, 8192, 2048), np.float32)
    for c in range(8):
        seq, span = c // 4, c % 4
        outp[seq, span * 2048:(span + 1) * 2048] = np.asarray(res.results[c]["out"])
    return outp
